# version 09
# baseline (speedup 1.0000x reference)
import numpy as np
import ml_dtypes
import sys as _sys
import concourse.bass as bass
import concourse.mybir as mybir
from concourse.bass_utils import run_bass_kernel_spmd

F32 = mybir.dt.float32
BF16 = mybir.dt.bfloat16
AF = mybir.ActivationFunctionType
ALU = mybir.AluOpType
AX = mybir.AxisListType.X

NCORES = 8
D = 2048
HF = 5632
HC = HF // 128
DC = D // 128
TC = 1024
SEQ = 4096
NB = 2
HD = 128
NW = HF * D
EPS_LN = 1e-5
EPS_RMS = 1e-5
EPS_Q = 1e-8
ISQ = 0.08838834764831845

_STAGE = 3
_TAPS = False


class Ring:
    def __init__(self, n):
        self.size = n
        self.rel = [None] * n
        self.i = 0

    def next(self):
        s = self.i % self.size
        self.i += 1
        return s, list(self.rel[s] or [])

    def release(self, s, waits):
        self.rel[s] = list(waits)


class Ctr:
    __slots__ = ("h", "name", "n")


class DmaRing:

    def __init__(self, B, queue, name, depth):
        self.B = B
        self.queue = queue
        self.sems = [B.new_ctr(f"{name}{i}") for i in range(depth)]
        self.rel = [None] * depth
        self.i = 0
        self.depth = depth

    def load(self, fn, waits=()):
        s = self.i % self.depth
        self.i += 1
        w = list(self.rel[s] or []) + list(waits)
        v = self.B.op(self.queue, fn, waits=w, inc=self.sems[s], inck=16)
        return s, [(self.sems[s], v)]

    def load_slot(self, s, fn, waits=()):
        v = self.B.op(self.queue, fn, waits=list(waits),
                      inc=self.sems[s], inck=16)
        return [(self.sems[s], v)]

    def release(self, s, waits):
        self.rel[s] = list(waits)

    def totals(self):
        return [(c, c.n) for c in self.sems]


class Builder:
    ENGS = ("sync", "scalar", "gpsimd", "tensor", "vector")

    def __init__(self, nc, stack):
        self.nc = nc
        self.stack = stack
        self.streams = {e: [] for e in self.ENGS}
        self.ctrs = {}
        for name in ("dW", "dA", "dG", "pe", "dve", "act", "pool", "cc"):
            self.new_ctr(name)

    def new_ctr(self, name):
        c = Ctr.__new__(Ctr)
        c.h = self.stack.enter_context(self.nc.semaphore(name))
        c.name = name
        c.n = 0
        self.ctrs[name] = c
        return c

    def op(self, eng, fn, waits=(), inc=None, inck=1, hwaits=()):
        w = [(c, v) for (c, v) in waits if c is not None and v > 0]
        hw = [(c, v) for (c, v) in hwaits if c is not None and v > 0]
        self.streams[eng].append((w, hw, fn, inc, inck))
        if inc is not None:
            inc.n += inck
            return inc.n
        return None

    def dma(self, queue, fn, waits=()):
        qc = {"sync": "dW", "scalar": "dA", "gpsimd": "dG"}[queue]
        return self.op(queue, fn, waits, inc=self.ctrs[qc], inck=16)

    def dma_on(self, queue, ctr, fn, waits=()):
        return self.op(queue, fn, waits, inc=ctr, inck=16)

    def fence(self):
        snap = [(c, c.n) for c in self.ctrs.values()]
        for eng in self.ENGS:
            self.op(eng, lambda e: None, waits=snap, hwaits=snap)

    def replay(self, eng_name, e):
        own = {"sync": None, "scalar": "act", "gpsimd": "pool",
               "tensor": "pe", "vector": "dve"}[eng_name]
        waited = {}
        for w, hw, fn, inc, inck in self.streams[eng_name]:
            for c, v in w:
                if c.name == own:
                    continue
                if waited.get(c.name, 0) >= v:
                    continue
                e.wait_ge(c.h, v)
                waited[c.name] = v
            for c, v in hw:
                if waited.get(c.name, 0) >= v:
                    continue
                e.wait_ge(c.h, v)
                waited[c.name] = v
            inst = fn(e)
            if inc is not None:
                inst.then_inc(inc.h, inck)


def build_program():
    nc = bass.Bass("TRN2", target_bir_lowering=False, debug=False,
                   detect_race_conditions=False)

    def din(name, shape, dt=F32):
        return nc.dram_tensor(name, shape, dt, kind="ExternalInput")

    xT = din("xT", [D, TC])
    w1T = din("w1T", [D, HF]); w2T = din("w2T", [D, HF]); w3T = din("w3T", [HF, D])
    g1s = din("g1s", [D, 768]); g2s = din("g2s", [D, 768])
    g3s = din("g3s", [768, D])
    wqT = din("wqT", [D, 256]); wkT = din("wkT", [D, 256]); wvT = din("wvT", [D, 256])
    woT = din("woT", [D, D])
    gq_in = din("gq", [HD, 1]); gk_in = din("gk", [HD, 1])
    maskB_in = din("maskB", [128, 2048], BF16)
    ones_b_in = din("ones_b", [128, 1], BF16)
    ones_f_in = din("ones_f", [128, 1])
    onesr_f_in = din("onesr_f", [1, 128])
    onesr_b_in = din("onesr_b", [1, 128], BF16)
    ide_in = din("ide", [128, 128])
    ideb_in = din("ide_b", [128, 128], BF16)

    y = nc.dram_tensor("y", [D, TC], F32, kind="ExternalOutput")
    if _TAPS:
        t_thp = nc.dram_tensor("t_thp", [128, 8], F32, kind="ExternalOutput")
        t_xln = nc.dram_tensor("t_xln", [128, DC * TC], BF16,
                               kind="ExternalOutput")
        t_w1q = nc.dram_tensor("t_w1q", [128, D], BF16, kind="ExternalOutput")
        t_w3q = nc.dram_tensor("t_w3q", [128, D], BF16, kind="ExternalOutput")
        t_hid = nc.dram_tensor("t_hid", [128, 512], BF16, kind="ExternalOutput")
        t_xa = nc.dram_tensor("t_xa", [128, 1024], F32, kind="ExternalOutput")

    agq_in = nc.dram_tensor("agq_in", [256, 3 * D], BF16, kind="Internal")
    agq_in_b = nc.dram_tensor("agq_in_b", [512, 3 * D], BF16, kind="Internal")
    agq_out_a = nc.dram_tensor("agq_out_a", [NCORES * 256, 3 * D], BF16,
                               kind="Internal", addr_space="Shared")
    agq_out_b = nc.dram_tensor("agq_out_b", [NCORES * 512, 3 * D], BF16,
                               kind="Internal", addr_space="Shared")
    agx_in = nc.dram_tensor("agx_in", [D, TC], BF16, kind="Internal")
    agx_out = nc.dram_tensor("agx_out", [NCORES * D, TC], BF16,
                             kind="Internal", addr_space="Shared")
    ago_in = nc.dram_tensor("ago_in", [256, NB * SEQ], BF16, kind="Internal")
    ago_out = nc.dram_tensor("ago_out", [D, NB * SEQ], BF16,
                             kind="Internal", addr_space="Shared")
    woS = nc.dram_tensor("woS", [D, D], BF16, kind="Internal")
    gar_in = nc.dram_tensor("gar_in", [1, 8], F32, kind="Internal")
    gar_out = nc.dram_tensor("gar_out", [1, 8], F32, kind="Internal")

    from contextlib import ExitStack
    stack = ExitStack()
    stack.__enter__()
    B = Builder(nc, stack)
    C = B.ctrs
    dW, dA, dG = C["dW"], C["dA"], C["dG"]
    pe, dve, act, pool, cc = C["pe"], C["dve"], C["act"], C["pool"], C["cc"]

    al = nc.alloc_sbuf_tensor
    X = al("X", [128, DC * TC], F32)
    XLN = al("XLN", [128, DC * TC], BF16)
    MB = al("MB", [128, 2048], BF16)
    IDE = al("IDE", [128, 128], F32)
    IDEB = al("IDEB", [128, 128], BF16)
    ONE_B = al("ONE_B", [128, 1], BF16)
    ONE_F = al("ONE_F", [128, 1], F32)
    ONER_F = al("ONER_F", [1, 128], F32)
    ONER_B = al("ONER_B", [1, 128], BF16)
    GQS = al("GQS", [128, 1], F32); GKS = al("GKS", [128, 1], F32)
    THP = al("THP", [128, 8], F32)
    GSUM = al("GSUM", [128, 4], F32)
    GROW = al("GROW", [1, 16], F32)
    COLS = al("COLS", [128, 24], F32)
    DIAG = al("DIAG", [128, 128], F32)
    TMPC = al("TMPC", [128, 1], F32)

    ARENA_BYTES = 88 * 1024
    arena = al("ARENA", [128, ARENA_BYTES], mybir.dt.uint8)
    abase = nc.lookup_mloc(arena).addr
    _aid = [0]

    def at(shape, dt, off):
        _aid[0] += 1
        return nc.alloc_sbuf_tensor_at(f"ar{_aid[0]}", shape, dt,
                                       offset=abase + off)

    STG = at([128, 2, 2048], F32, 0)
    U2S = at([128, 2048], BF16, 16384)
    W1Q = at([128, 2, 2048], BF16, 20480)
    W2Q = at([128, 2, 2048], BF16, 28672)
    W3Q = at([128, 2, 1024], BF16, 36864)
    HID = at([128, HC, 512], BF16, 40960)
    SG = at([128, 2, 512], BF16, 86016)

    TMPF = al("TMPF", [128, 512], F32)
    X2S = al("X2S", [128, 1024], BF16)
    MBR = al("MBR", [128, 1024], BF16)
    RBR = al("RBR", [128, 1024], BF16)


    GST = STG

    QA = at([128, SEQ], BF16, 0)
    QB = at([128, SEQ], BF16, 8192)
    KA = at([128, SEQ], BF16, 16384)
    KB = at([128, SEQ], BF16, 24576)
    VV = at([128, 2 * SEQ], BF16, 32768)
    XGR = at([128, 6, 512], BF16, 49152)
    PBR = at([128, 2, 1024], BF16, 55296)
    PSB = at([128, 2, 512], BF16, 59392)
    Q2S = at([128, 2, 512], BF16, 61440)
    OSTB = at([128, 2, 512], BF16, 63488)
    LBS = at([128, 512], F32, 65536)
    LROW = at([1, 512], F32, 67584)
    LRB = at([1, 512], BF16, 69632)
    R0 = at([1, 1024], F32, 70656)
    R1 = at([1, 1024], F32, 74752)
    RSR = at([1, 512], F32, 78848)
    RSB = at([1, 512], BF16, 80896)
    XBH = at([128, 2, 512], BF16, 61440)
    R2 = at([1, 1024], F32, 81920)

    xlnb = nc.lookup_mloc(XLN).addr
    WQ = nc.alloc_sbuf_tensor_at("WQ", [128, DC, 256], BF16, offset=xlnb)
    WK = nc.alloc_sbuf_tensor_at("WK", [128, DC, 256], BF16, offset=xlnb + 8192)
    WV = nc.alloc_sbuf_tensor_at("WV", [128, DC, 256], BF16,
                                 offset=xlnb + 16384)

    OM = at([128, DC, TC], BF16, 0)
    WOS = at([128, 4, 2048], BF16, 32768)

    PS1 = nc.alloc_psum_tensor("PS1", [128, 2048], F32)
    PS2 = nc.alloc_psum_tensor("PS2", [128, 2048], F32)

    def b1(i):
        return PS1[:, 512 * i:512 * (i + 1)]

    def b2(i):
        return PS2[:, 512 * i:512 * (i + 1)]

    def MM(out_ap, lhsT_ap, rhs_ap, start, stop, waits=(), inc=None):
        return B.op("tensor",
                    lambda e, o=out_ap, l=lhsT_ap, r=rhs_ap, s=start, t=stop:
                    e.matmul(o, l, r, start=s, stop=t), waits, inc=inc)

    def xsl(dc, tq):
        return X[:, dc * TC + tq * 512: dc * TC + (tq + 1) * 512]

    def xlsl(dc, tq):
        return XLN[:, dc * TC + tq * 512: dc * TC + (tq + 1) * 512]

    dx = B.new_ctr("dx")
    for dc in range(DC):
        B.dma_on("sync", dx, lambda e, dc=dc: e.dma_start(
            out=X[:, dc * TC:(dc + 1) * TC], in_=xT[dc * 128:(dc + 1) * 128, :]))
    x_loaded = dx.n
    xw = (dx, x_loaded)

    dcst = B.new_ctr("dcst")
    for dst, src in ((MB, maskB_in), (IDE, ide_in),
                     (IDEB, ideb_in), (ONER_B, onesr_b_in),
                     (ONE_B, ones_b_in), (ONE_F, ones_f_in),
                     (ONER_F, onesr_f_in), (GQS, gq_in), (GKS, gk_in)):
        B.dma_on("scalar", dcst, lambda e, d=dst, s=src:
                 e.dma_start(out=d[:], in_=s[:]))
    const_loaded = dcst.n
    cw = (dcst, const_loaded)
    B.op("vector", lambda e: e.tensor_scalar_mul(GQS[:], GQS[:], ISQ),
         waits=[cw], inc=dve)
    B.op("vector", lambda e: e.memset(GROW[:], 0.0), inc=dve)
    B.op("vector", lambda e: e.memset(THP[:], 0.0), inc=dve)
    B.op("vector", lambda e: e.memset(THP[:, 3:4], -0.5), inc=dve)
    B.op("vector", lambda e: e.memset(THP[:, 7:8], -20.0), inc=dve)

    def emit_gamma():
        ring = DmaRing(B, "sync", "gst", 2)
        dgar = B.new_ctr("dgar")
        plan = []
        for j, (g, rows, cols) in enumerate(
                [(g1s, D, 768), (g2s, D, 768), (g3s, 768, D)]):
            per = max(1, 2048 // cols)
            r0 = 0
            while r0 < rows // 128:
                n = min(per, rows // 128 - r0)
                plan.append((g, r0, n, cols, j))
                r0 += n
        jcols = {0: [], 1: [], 2: []}
        for ci, (g, r0, n, cols, j) in enumerate(plan):
            def ldfn(e, g=g, r0=r0, n=n, cols=cols, s=None):
                return e.dma_start(
                    out=GST[:, s, 0:n * cols].rearrange("p (a h) -> p a h", a=n),
                    in_=g[r0 * 128:(r0 + n) * 128, :].rearrange(
                        "(a p) h -> p a h", p=128))
            slot = ring.i % ring.depth
            _, ldw = ring.load(
                lambda e, f=ldfn, s=slot: f(e, s=s))
            rd = B.op("vector", lambda e, s=slot, n=n, cols=cols, ci=ci:
                      e.reduce_sum(COLS[:, ci:ci + 1], GST[:, s, 0:n * cols],
                                   AX, apply_absolute_value=True),
                      waits=ldw, inc=dve)
            jcols[j].append(ci)
            ring.release(slot, [(dve, rd)])
        allred = dve.n
        for j in range(3):
            c0, c1 = jcols[j][0], jcols[j][-1] + 1
            gs = B.op("vector", lambda e, j=j, c0=c0, c1=c1: e.reduce_sum(
                GSUM[:, j:j + 1], COLS[:, c0:c1], AX),
                hwaits=[(dve, allred)], inc=dve)
            dsn = B.op("vector", lambda e, j=j: e.tensor_scalar_mul(
                DIAG[:], IDE[:], GSUM[:, j:j + 1]),
                waits=[cw, (pe, pe.n)],
                hwaits=[(dve, gs)], inc=dve)
            msn = MM(PS1[0:1, 128 * j:128 * j + 128], ONE_F[:], DIAG[:],
                     True, True, waits=[(dve, dsn)], inc=pe)
            B.op("vector", lambda e, j=j: e.reduce_sum(
                GROW[0:1, j:j + 1], PS1[0:1, 128 * j:128 * j + 128], AX),
                waits=[(pe, msn)], inc=dve)
        gsn = dve.n
        s1 = B.dma_on("gpsimd", dgar, lambda e: e.dma_start(
            out=gar_in[:, :], in_=GROW[0:1, 0:8]), waits=[(dve, gsn)])
        c1 = B.op("gpsimd", lambda e: e.collective_compute(
            "AllReduce", ALU.add, replica_groups=[list(range(NCORES))],
            ins=[gar_in[:, :].opt()], outs=[gar_out[:, :].opt()]),
            waits=[(dgar, s1)], inc=cc)
        s2 = B.dma_on("gpsimd", dgar, lambda e: e.dma_start(
            out=GROW[0:1, 8:16], in_=gar_out[:, :]), waits=[(cc, c1)])
        tsn = B.op("vector", lambda e: e.tensor_scalar(
            GROW[0:1, 8:16], GROW[0:1, 8:16], 0.5 / NW, 0.5 * EPS_Q,
            ALU.mult, ALU.add), waits=[(dgar, s2)], inc=dve)
        for j in range(3):
            msn = MM(PS1[:, 512 + j:513 + j], ONER_F[:], GROW[0:1, 8 + j:9 + j],
                     True, True, waits=[(dve, tsn)], inc=pe)
            B.op("vector", lambda e, j=j: e.tensor_copy(
                THP[:, j:j + 1], PS1[:, 512 + j:513 + j]),
                waits=[(pe, msn)], inc=dve)
            B.op("vector", lambda e, j=j: e.tensor_scalar_mul(
                THP[:, 4 + j:5 + j], PS1[:, 512 + j:513 + j], -1.0), inc=dve)
        return dve.n

    gamma_done = emit_gamma()

    def emit_layernorm(prereq):
        prev_mm = {}
        for i, (dc, tq) in enumerate(
                [(d_, t_) for d_ in range(DC) for t_ in range(2)]):
            cs = B.op("scalar", lambda e, dc=dc, tq=tq, i=i: e.activation(
                XBH[:, i % 2, :], xsl(dc, tq), AF.Identity),
                waits=list(prereq) + [(pe, prev_mm.get(i - 2, 0))], inc=act)
            sq = B.op("scalar", lambda e, dc=dc, tq=tq, i=i: e.activation(
                X2S[:, (i % 2) * 512:(i % 2 + 1) * 512], xsl(dc, tq),
                AF.Square), inc=act)
            MM(PS2[0:1, 1024 + tq * 512:1536 + tq * 512], ONE_B[:],
               XBH[:, i % 2, :], start=(dc == 0), stop=(dc == DC - 1),
               waits=[(act, cs)])
            m = MM(PS1[0:1, 1024 + tq * 512:1536 + tq * 512], ONE_B[:],
                   X2S[:, (i % 2) * 512:(i % 2 + 1) * 512],
                   start=(dc == 0), stop=(dc == DC - 1),
                   waits=[(act, sq)], inc=pe)
            prev_mm[i] = m
        stat_done = pe.n
        s1 = B.op("vector", lambda e: e.tensor_scalar_mul(
            R0[:, :], PS2[0:1, 1024:2048], 1.0 / D),
            waits=[(pe, stat_done)], inc=dve)
        s2 = B.op("vector", lambda e: e.tensor_scalar_mul(
            R1[:, :], PS1[0:1, 1024:2048], 1.0 / D), inc=dve)
        s3 = B.op("vector", lambda e: e.tensor_mul(
            R2[:, :], R0[:, :], R0[:, :]),
            hwaits=[(dve, s1)], inc=dve)
        s4 = B.op("vector", lambda e: e.tensor_sub(
            R1[:, :], R1[:, :], R2[:, :]),
            hwaits=[(dve, s3)], inc=dve)
        vsn = B.op("vector", lambda e: e.tensor_scalar_add(
            R1[:, :], R1[:, :], EPS_LN), hwaits=[(dve, s4)], inc=dve)
        ssn = B.op("scalar", lambda e: e.activation(R1[:, :], R1[:, :], AF.Ln),
                   waits=[(dve, vsn)], inc=act)
        rsn2 = B.op("scalar", lambda e: e.activation(
            R1[:, :], R1[:, :], AF.Exp, scale=-0.5), inc=act)
        rsn = B.op("vector", lambda e: e.engine_nop(),
                   waits=[(act, rsn2)], inc=dve)
        for tq in range(2):
            m1 = MM(b2(tq), ONER_F[:], R0[0:1, tq * 512:(tq + 1) * 512],
                    True, True, waits=[(dve, rsn)], inc=pe)
            B.op("vector", lambda e, tq=tq: e.tensor_copy(
                MBR[:, tq * 512:(tq + 1) * 512], b2(tq)),
                waits=[(pe, m1)], inc=dve)
            m2 = MM(b2(2 + tq), ONER_F[:], R1[0:1, tq * 512:(tq + 1) * 512],
                    True, True, inc=pe)
            B.op("vector", lambda e, tq=tq: e.tensor_copy(
                RBR[:, tq * 512:(tq + 1) * 512], b2(2 + tq)),
                waits=[(pe, m2)], inc=dve)
        bc = dve.n
        for dc in range(DC):
            for tq in range(2):
                B.op("vector", lambda e, dc=dc, tq=tq: e.tensor_sub(
                    TMPF[:, :], xsl(dc, tq),
                    MBR[:, tq * 512:(tq + 1) * 512]),
                    waits=[(dve, bc)], inc=dve)
                B.op("vector", lambda e, dc=dc, tq=tq: e.tensor_mul(
                    xlsl(dc, tq), TMPF[:, :],
                    RBR[:, tq * 512:(tq + 1) * 512]), inc=dve)
        return dve.n

    stg = Ring(2)
    stgd = DmaRing(B, "sync", "stg", 2)
    qsc = DmaRing(B, "scalar", "qsc", 2)
    agq_ready = [0, 0]
    qsc_early = [[]]

    QOBT = al("QOBT", [128, 2, 2048], BF16)

    def emit_qshard():
        QOB = [QOBT[:, 0, :], QOBT[:, 1, :]]
        SGN = U2S[:, :]
        prev_q = [[], []]
        for u in range(6):
            for j, g in enumerate((g1s, g2s, g3s)):
                ss = stgd.i % stgd.depth
                if j == 2:
                    _, ldw = stgd.load(lambda e, u=u, s=ss: e.dma_start(
                        out=STG[:, s, :],
                        in_=g3s[u * 128:(u + 1) * 128, :]),
                        waits=[(dve, gamma_done)])
                else:
                    _, ldw = stgd.load(lambda e, g=g, u=u, s=ss: e.dma_start(
                        out=STG[:, s, :].rearrange("p (a h) -> p a h", a=16),
                        in_=g[:, u * 128:(u + 1) * 128].rearrange(
                            "(a p) h -> p a h", p=128)),
                        waits=[(dve, gamma_done)])
                qi = (u * 3 + j) % 2
                u1 = B.op("vector", lambda e, s=ss, j=j, qi=qi:
                          e.tensor_scalar(
                    QOB[qi], STG[:, s, :], THP[:, j:j + 1], THP[:, 3:4],
                    ALU.is_ge, ALU.add),
                    waits=ldw + [(dve, gamma_done)] + prev_q[qi], inc=dve)
                sg = B.op("scalar", lambda e, s=ss, j=j: e.activation(
                    SGN, STG[:, s, :], AF.Sign, bias=THP[:, j:j + 1]),
                    waits=ldw + [(dve, _qd_used[0])], inc=act)
                qd = B.op("vector", lambda e, qi=qi: e.scalar_tensor_tensor(
                    QOB[qi], SGN, 0.5, QOB[qi], ALU.mult, ALU.add),
                    waits=[(act, sg)], inc=dve)
                _qd_used[0] = qd
                st = qsc.load_slot(qi, lambda e, qi=qi, u=u, j=j: e.dma_start(
                    out=(agq_in[u * 128:(u + 1) * 128, j * D:(j + 1) * D]
                         if u < 2 else
                         agq_in_b[(u - 2) * 128:(u - 1) * 128,
                                  j * D:(j + 1) * D]),
                    in_=QOB[qi]), waits=[(dve, qd)])
                prev_q[qi] = st
                stgd.release(ss, [(dve, u1), (act, sg)])
            if u == 1:
                qsc_early[0] = qsc.totals()
        aga = B.op("gpsimd", lambda e: e.collective_compute(
            "AllGather", ALU.bypass, replica_groups=[list(range(NCORES))],
            ins=[agq_in[:, :].opt()], outs=[agq_out_a[:, :].opt()]),
            waits=qsc_early[0], inc=cc)
        agb = B.op("gpsimd", lambda e: e.collective_compute(
            "AllGather", ALU.bypass, replica_groups=[list(range(NCORES))],
            ins=[agq_in_b[:, :].opt()], outs=[agq_out_b[:, :].opt()]),
            waits=qsc.totals(), inc=cc)
        agq_ready[0] = aga
        agq_ready[1] = agb

    _qd_used = [0]
    _sgn_used = [0]

    r1 = Ring(2)
    r2 = Ring(2)
    w1l = DmaRing(B, "sync", "w1l", 2)
    w2l = DmaRing(B, "sync", "w2l", 2)
    w3l = DmaRing(B, "sync", "w3l", 2)
    r3 = Ring(2)
    p2evac = [0]

    def qsrc(hc):
        if hc < 16:
            return agq_out_a, (hc % 8) * 256 + (hc // 8) * 128
        return agq_out_b, (hc % 8) * 512 + (hc // 8 - 2) * 128

    def emit_ffn(first, ln_done):
        sig_done = {}
        gate_done = {}
        evacs_prev = [p2evac[0]]
        for tq in range(2):
            sig_done.clear()
            gate_done.clear()
            for hc in range(HC):
                consume = {}
                for widx, wqbuf, wring, ldr in (
                        (0, W1Q, r1, w1l), (1, W2Q, r2, w2l)):
                    qs, qw = wring.next()
                    ldw = ldr.load_slot(qs, lambda e, hc=hc, q=qs, wq=wqbuf,
                                        widx=widx: e.dma_start(
                        out=wq[:, q, :],
                        in_=qsrc(hc)[0][qsrc(hc)[1]:qsrc(hc)[1] + 128,
                                        widx * D:(widx + 1) * D]),
                        waits=list(qw) + [(cc, agq_ready[0 if hc < 16 else 1])])
                    consume[widx] = (qs, ldw)
                xb = hc % 2
                q1, w1w = consume[0]
                q2, w2w = consume[1]
                for dc in range(DC):
                    MM(b1(xb), W1Q[:, q1, dc * 128:(dc + 1) * 128],
                       xlsl(dc, tq), start=(dc == 0), stop=(dc == DC - 1),
                       waits=(w1w + [(dve, ln_done),
                                     (act, sig_done.get(hc - 2, 0)),
                                     (dve, evacs_prev[0])])
                       if dc == 0 else (),
                       inc=(pe if dc == DC - 1 else None))
                x1g = pe.n
                for dc in range(DC):
                    MM(b1(2 + xb), W2Q[:, q2, dc * 128:(dc + 1) * 128],
                       xlsl(dc, tq), start=(dc == 0), stop=(dc == DC - 1),
                       waits=(w2w + [(dve, gate_done.get(hc - 2, 0)),
                                     (dve, evacs_prev[0])])
                       if dc == 0 else (),
                       inc=(pe if dc == DC - 1 else None))
                x2g = pe.n
                r1.release(q1, [(pe, x1g)])
                r2.release(q2, [(pe, x2g)])
                sg = B.op("scalar", lambda e, xb=xb: e.activation(
                    SG[:, xb, :], b1(xb), AF.Sigmoid),
                    waits=[(pe, x1g), (dve, gate_done.get(hc - 2, 0))],
                    inc=act)
                hd_ = B.op("vector", lambda e, xb=xb, hc=hc: e.tensor_tensor(
                    HID[:, hc, :], b1(2 + xb), SG[:, xb, :], ALU.mult),
                    waits=[(pe, x2g), (act, sg)], inc=dve)
                sig_done[hc] = sg
                gate_done[hc] = hd_
            p1_dve = dve.n
            for dtb in range(2):
                grp = 0
                for hc in range(HC):
                    s3, w3w_ = r3.next()
                    ldw = w3l.load_slot(s3, lambda e, hc=hc, dtb=dtb, s=s3:
                                        e.dma_start(
                        out=W3Q[:, s, :],
                        in_=qsrc(hc)[0][qsrc(hc)[1]:qsrc(hc)[1] + 128,
                                        2 * D + dtb * 1024:
                                        2 * D + (dtb + 1) * 1024]),
                        waits=list(w3w_) + [(cc, agq_ready[0 if hc < 16 else 1])])
                    for dtl in range(8):
                        ps = b1(dtl) if dtl < 4 else b2(dtl - 4)
                        grp = MM(ps, W3Q[:, s3, dtl * 128:(dtl + 1) * 128],
                                 HID[:, hc, :],
                                 start=(hc == 0), stop=(hc == HC - 1),
                                 waits=(ldw + [(dve, p1_dve),
                                               (dve, evacs_prev[0])])
                                 if dtl == 0 else ldw,
                                 inc=(pe if dtl == 7 else None))
                    r3.release(s3, [(pe, grp)])
                for dtl in range(8):
                    dt = dtb * 8 + dtl
                    ps = b1(dtl) if dtl < 4 else b2(dtl - 4)
                    B.op("vector", lambda e, dt=dt, ps=ps, tq=tq:
                         e.scalar_tensor_tensor(
                        xsl(dt, tq), ps, 0.5, xsl(dt, tq), ALU.mult, ALU.add),
                        waits=[(pe, grp)], inc=dve)
                evacs_prev[0] = dve.n
        p2evac[0] = dve.n
        return dve.n

    def emit_attention(ln2_done):
        dagx = B.new_ctr("dagx")
        for dc in range(DC):
            B.dma_on("scalar", dagx, lambda e, dc=dc: e.dma_start(
                out=agx_in[dc * 128:(dc + 1) * 128, :],
                in_=XLN[:, dc * TC:(dc + 1) * TC]), waits=[(dve, ln2_done)])
        contrib = dagx.n
        ag1 = B.op("gpsimd", lambda e: e.collective_compute(
            "AllGather", ALU.bypass, replica_groups=[list(range(NCORES))],
            ins=[agx_in[:, :].opt()], outs=[agx_out[:, :].opt()]),
            waits=[(dagx, contrib)], inc=cc)
        dwqkv = B.new_ctr("dwqkv")
        for w_in, wbuf in ((wqT, WQ), (wkT, WK), (wvT, WV)):
            B.dma_on("gpsimd", dwqkv, lambda e, w_in=w_in, wbuf=wbuf:
                     e.dma_start(
                out=wbuf[:], in_=w_in.rearrange("(a p) c -> p a c", p=128)),
                waits=[(dagx, contrib)])
        B.dma_on("gpsimd", dwqkv, lambda e: e.dma_start(
            out=woS[:, :], in_=woT[:, :]), waits=[(dagx, contrib)])
        wload = (dwqkv, dwqkv.n)

        xg = DmaRing(B, "scalar", "xg", 6)
        ostw = DmaRing(B, "gpsimd", "ostw", 2)
        scr_ring = Ring(3)
        pb_ring = Ring(2)
        ost_ring = Ring(2)
        passA_credit = [None, None]
        passB_credit = [None, None, None]
        qk_ready = [0]
        v_ready = [0]
        prev_rc = [0]
        prev_bc = [0]
        nmr_used = [0]
        lbs_used = [0]
        lrow_done = [0]
        ov_done = [0]
        pb_prev = [None]
        pair_cnt = [0]
        pair_credit = [None, None]
        psb_i = [0]
        psb_rel = [None, None]

        def emit_proj(b):
            B.fence()
            prev_evac = {}
            vprev = 0
            for tq8 in range(8):
                vlast = 0
                for dc in range(DC):
                    s = xg.i % xg.depth
                    _, ldw = xg.load(lambda e, b=b, tq8=tq8, dc=dc, s=s:
                                     e.dma_start(
                        out=XGR[:, s, :],
                        in_=agx_out[(4 * b + tq8 // 2) * D + dc * 128:
                                    (4 * b + tq8 // 2) * D + (dc + 1) * 128,
                                    (tq8 % 2) * 512:(tq8 % 2 + 1) * 512]),
                        waits=[(cc, ag1)])
                    w0 = ldw + [wload]
                    if dc == 0:
                        for t in range(4):
                            w0 += prev_evac.get(t, [])
                        if vprev:
                            w0 += [(act, vprev)]
                    MM(b1(0), WQ[:, dc, 0:128], XGR[:, s, :],
                       start=(dc == 0), stop=(dc == DC - 1), waits=w0)
                    MM(b1(1), WQ[:, dc, 128:256], XGR[:, s, :],
                       start=(dc == 0), stop=(dc == DC - 1))
                    MM(b1(2), WK[:, dc, 0:128], XGR[:, s, :],
                       start=(dc == 0), stop=(dc == DC - 1))
                    MM(b1(3), WK[:, dc, 128:256], XGR[:, s, :],
                       start=(dc == 0), stop=(dc == DC - 1))
                    for t4 in range(4):
                        vlast = MM(PS2[:, t4 * 256:(t4 + 1) * 256],
                                   XGR[:, s, t4 * 128:(t4 + 1) * 128],
                                   WV[:, dc, :],
                                   start=(dc == 0), stop=(dc == DC - 1),
                                   inc=(pe if t4 == 3 else None))
                    xg.release(s, [(pe, vlast)])
                grp = vlast
                prev_rw = 0
                prev_ap = 0
                prev_rsmm = {}
                for t, (psb, dbuf, gcol) in enumerate(
                        ((b1(0), QA, GQS), (b1(1), QB, GQS),
                         (b1(2), KA, GKS), (b1(3), KB, GKS))):
                    sq = B.op("scalar", lambda e, psb=psb, t=t: e.activation(
                        Q2S[:, t % 2, :], psb, AF.Square),
                        waits=[(pe, grp), (pe, prev_rsmm.get(t - 2, 0))],
                        inc=act)
                    cp = B.op("scalar", lambda e, psb=psb, dbuf=dbuf, tq8=tq8:
                              e.activation(
                        dbuf[:, tq8 * 512:(tq8 + 1) * 512], psb, AF.Identity),
                        inc=act)
                    mm = MM(PS2[0:1, 1024:1536], ONE_B[:], Q2S[:, t % 2, :],
                            True, True,
                            waits=[(act, sq), (dve, prev_rw)], inc=pe)
                    prev_rsmm[t] = mm
                    rw = B.op("vector", lambda e: e.tensor_scalar(
                        RSR[0:1, :], PS2[0:1, 1024:1536],
                        1.0 / HD, EPS_RMS, ALU.mult, ALU.add),
                        waits=[(pe, mm), (act, prev_rc[0])], inc=dve)
                    sr = B.op("scalar", lambda e: e.activation(
                        RSR[0:1, :], RSR[0:1, :], AF.Ln),
                        waits=[(dve, rw)], inc=act)
                    rc = B.op("scalar", lambda e: e.activation(
                        RSB[0:1, :], RSR[0:1, :], AF.Exp, scale=-0.5),
                        waits=[(pe, prev_bc[0])], inc=act)
                    prev_rc[0] = rc
                    bc = MM(PS2[:, 1536:2048], ONER_B[:],
                            RSB[0:1, :], True, True,
                            waits=[(act, rc), (dve, prev_ap)], inc=pe)
                    prev_bc[0] = bc
                    ap = B.op("vector", lambda e, dbuf=dbuf, tq8=tq8, gcol=gcol:
                              e.scalar_tensor_tensor(
                        dbuf[:, tq8 * 512:(tq8 + 1) * 512],
                        dbuf[:, tq8 * 512:(tq8 + 1) * 512], gcol[:],
                        PS2[:, 1536:2048], ALU.mult, ALU.mult),
                        waits=[(pe, bc), (act, cp)], inc=dve)
                    prev_rw = rw
                    prev_ap = ap
                    prev_evac[t] = [(dve, ap), (act, cp)]
                for t4 in range(4):
                    vprev = B.op("scalar", lambda e, tq8=tq8, t4=t4:
                                 e.activation(
                        VV[:, (tq8 * 4 + t4) * 256:(tq8 * 4 + t4 + 1) * 256],
                        PS2[:, t4 * 256:(t4 + 1) * 256], AF.Identity),
                        waits=[(pe, grp)], inc=act)
            qk_ready[0] = dve.n
            v_ready[0] = act.n

        def emit_passB_qg(b, h, qg):
            Q = (QA, QB)[h]
            K = (KA, KB)[h]
            nkc = 4 * qg + 4
            pend = None
            grp = 0
            for pr in range(nkc // 2):
                half = pair_cnt[0] % 2
                pair_cnt[0] += 1
                hw0 = [(dve, qk_ready[0])] \
                    + ([pair_credit[half]] if pair_credit[half] else [])
                for sub in range(2):
                    kc = 2 * pr + sub
                    bank = b2(half * 2 + sub)
                    w = hw0 if sub == 0 else ()
                    if kc >= 4 * qg:
                        v = kc - 4 * qg
                        MM(bank, IDEB[:], MB[:, v * 512:(v + 1) * 512],
                           True, False, waits=w)
                        mm = MM(bank, K[:, kc * 128:(kc + 1) * 128],
                                Q[:, qg * 512:(qg + 1) * 512], False, True,
                                inc=(pe if sub == 1 else None))
                    else:
                        mm = MM(bank, K[:, kc * 128:(kc + 1) * 128],
                                Q[:, qg * 512:(qg + 1) * 512], True, True,
                                waits=w, inc=(pe if sub == 1 else None))
                if pb_prev[0] is not None:
                    pb_ring.release(pb_prev[0], [(pe, mm)])
                ps_, pw = pb_ring.next()
                ex = B.op("scalar", lambda e, half=half, p=ps_: e.activation(
                    PBR[:, p, :], PS2[:, half * 1024:(half + 1) * 1024],
                    AF.Exp, bias=THP[:, 7:8]),
                    waits=[(pe, mm)] + pw, inc=act)
                pair_credit[half] = (act, ex)
                sb = psb_i[0] % 2
                psb_i[0] += 1
                sm = B.op("vector", lambda e, p=ps_, sb=sb: e.tensor_add(
                    PSB[:, sb, :], PBR[:, p, 0:512], PBR[:, p, 512:1024]),
                    waits=[(act, ex)] + (psb_rel[sb] or []), inc=dve)
                if pend is not None:
                    pps, ppr, psb_, psm = pend
                    MM(PS1[0:1, 1536:2048], ONE_B[:], PSB[:, psb_, :],
                       start=(ppr == 0), stop=False,
                       waits=[(dve, psm)]
                       + ([(dve, lbs_used[0])] if ppr == 0 else []))
                    psb_rel[psb_] = [(pe, pe.n)]
                    for s2 in range(2):
                        pkc = 2 * ppr + s2
                        MM(PS1[:, 1024:1536],
                           VV[:, pkc * 256 + h * 128: pkc * 256 + h * 128 + 128],
                           PBR[:, pps, s2 * 512:(s2 + 1) * 512],
                           start=(pkc == 0), stop=False,
                           waits=([(dve, ov_done[0]), (act, v_ready[0])]
                                  if pkc == 0 else ()))
                pend = (ps_, pr, sb, sm)
            pps, ppr, psb_, psm = pend
            MM(PS1[0:1, 1536:2048], ONE_B[:], PSB[:, psb_, :],
               start=(ppr == 0), stop=True,
               waits=[(dve, psm)] + ([(dve, lbs_used[0])] if ppr == 0 else []))
            psb_rel[psb_] = [(pe, pe.n)]
            for s2 in range(2):
                pkc = 2 * ppr + s2
                grp = MM(PS1[:, 1024:1536],
                         VV[:, pkc * 256 + h * 128: pkc * 256 + h * 128 + 128],
                         PBR[:, pps, s2 * 512:(s2 + 1) * 512],
                         start=(pkc == 0), stop=(s2 == 1),
                         waits=([(dve, ov_done[0]), (act, v_ready[0])]
                                if pkc == 0 else ()),
                         inc=(pe if s2 == 1 else None))
            pb_ring.release(pps, [(pe, grp)])
            pb_prev[0] = None
            lr = B.op("vector", lambda e: e.tensor_scalar_add(
                LROW[0:1, :], PS1[0:1, 1536:2048], 1e-38),
                waits=[(pe, grp)], inc=dve)
            lrow_done[0] = lr
            l1 = B.op("scalar", lambda e: e.activation(
                LROW[0:1, :], LROW[0:1, :], AF.Ln), waits=[(dve, lr)], inc=act)
            rc = B.op("scalar", lambda e: e.activation(
                LRB[0:1, :], LROW[0:1, :], AF.Exp, scale=-1.0), inc=act)
            lb = MM(PS1[:, 1536:2048], ONER_B[:], LRB[0:1, :], True, True,
                    waits=[(act, rc), (dve, lbs_used[0]), (dve, lr)], inc=pe)
            lv = B.op("vector", lambda e: e.tensor_copy(
                LBS[:, :], PS1[:, 1536:2048]), waits=[(pe, lb)], inc=dve)
            lbs_used[0] = lv
            os_, ow = ost_ring.next()
            ov = B.op("vector", lambda e, o=os_: e.tensor_tensor(
                OSTB[:, o, :], PS1[:, 1024:1536], LBS[:, :], ALU.mult),
                waits=[(pe, grp)] + ow, inc=dve)
            ov_done[0] = ov
            odw = ostw.load_slot(os_, lambda e, o=os_, b=b, h=h, qg=qg:
                                 e.dma_start(
                out=ago_in[h * 128:(h + 1) * 128,
                           b * SEQ + qg * 512: b * SEQ + (qg + 1) * 512],
                in_=OSTB[:, o, :]), waits=[(dve, ov)])
            ost_ring.release(os_, odw)

        for b in range(NB):
            emit_proj(b)
            for h in range(2):
                for qg in range(8):
                    emit_passB_qg(b, h, qg)

        ag2 = B.op("gpsimd", lambda e: e.collective_compute(
            "AllGather", ALU.bypass, replica_groups=[list(range(NCORES))],
            ins=[ago_in[:, :].opt()], outs=[ago_out[:, :].opt()]),
            waits=ostw.totals(), inc=cc)
        B.fence()
        pid_cell = []
        dom = B.new_ctr("dom")

        def om_read(e, dv):
            if not pid_cell:
                pid_cell.append(e.partition_id())
            return e.dma_start(
                out=OM[:, dv, :],
                in_=ago_out[dv * 128:(dv + 1) * 128, bass.ts(pid_cell[0], TC)])

        for dv in range(DC):
            B.dma_on("gpsimd", dom, lambda e, dv=dv: om_read(e, dv),
                     waits=[(cc, ag2)])
        om_done = (dom, dom.n)
        wos = DmaRing(B, "sync", "wos", 4)
        wo_evac = [p2evac[0]]
        for dog in range(4):
            grp = 0
            for dv in range(DC):
                s = wos.i % wos.depth
                _, ldw = wos.load(lambda e, dv=dv, s=s: e.dma_start(
                    out=WOS[:, s, :], in_=woS[dv * 128:(dv + 1) * 128, :]),
                    waits=[om_done, wload])
                for dol in range(4):
                    do = dog * 4 + dol
                    for tq in range(2):
                        bk = dol * 2 + tq
                        ps = b1(bk) if bk < 4 else b2(bk - 4)
                        grp = MM(ps, WOS[:, s, do * 128:(do + 1) * 128],
                                 OM[:, dv, tq * 512:(tq + 1) * 512],
                                 start=(dv == 0), stop=(dv == DC - 1),
                                 waits=(ldw + [(dve, wo_evac[0])]
                                        if (dol == 0 and tq == 0) else ldw),
                                 inc=(pe if (dol == 3 and tq == 1) else None))
                wos.release(s, [(pe, grp)])
            for dol in range(4):
                do = dog * 4 + dol
                for tq in range(2):
                    bk = dol * 2 + tq
                    ps = b1(bk) if bk < 4 else b2(bk - 4)
                    B.op("vector", lambda e, do=do, tq=tq, ps=ps:
                         e.tensor_tensor(
                        xsl(do, tq), ps, xsl(do, tq), ALU.add),
                        waits=[(pe, grp)], inc=dve)
            wo_evac[0] = dve.n
        p2evac[0] = dve.n
        return dve.n

    emit_qshard()
    ln1 = emit_layernorm([xw])
    if _TAPS:
        B.dma("gpsimd", lambda e: e.dma_start(out=t_thp[:, :], in_=THP[:]),
              waits=[(dve, gamma_done)])
        B.dma("gpsimd", lambda e: e.dma_start(out=t_xln[:, :], in_=XLN[:]),
              waits=[(dve, ln1)])
    out_src = emit_ffn(True, ln1)
    if _TAPS:
        B.dma("gpsimd", lambda e: e.dma_start(out=t_xa[:, :], in_=X[:, 0:1024]),
              waits=[(dve, _dtb0_snap[0])])
        B.dma("gpsimd", lambda e: e.dma_start(out=t_w1q[:, :], in_=w1s[0, :, :]),
              waits=[(dve, out_src)])
        B.dma("gpsimd", lambda e: e.dma_start(out=t_w3q[:, :], in_=w3s[0, :, :]),
              waits=[(dve, out_src)])
        B.dma("gpsimd", lambda e: e.dma_start(out=t_hid[:, :], in_=HID[:, 0, :]),
              waits=[(dve, out_src)])
    if _STAGE >= 2:
        B.fence()
        ln2 = emit_layernorm([(dve, out_src)])
        out_src = emit_attention(ln2)
    if _STAGE >= 3:
        B.fence()
        ln3 = emit_layernorm([(dve, out_src)])
        out_src = emit_ffn(False, ln3)

    final_dve = dve.n
    dy = B.new_ctr("dy")
    for dc in range(DC):
        B.dma_on("sync", dy, lambda e, dc=dc: e.dma_start(
            out=y[dc * 128:(dc + 1) * 128, :],
            in_=X[:, dc * TC:(dc + 1) * TC]), waits=[(dve, final_dve)])
    out_done = dy.n
    B.op("sync", lambda e: e.wait_ge(dy.h, out_done))
    if _TAPS:
        B.op("gpsimd", lambda e: e.wait_ge(dG.h, dG.n))

    with nc.Block() as block:
        block.sync(lambda e: B.replay("sync", e))
        block.scalar(lambda e: B.replay("scalar", e))
        block.gpsimd(lambda e: B.replay("gpsimd", e))
        block.tensor(lambda e: B.replay("tensor", e))
        block.vector(lambda e: B.replay("vector", e))

    stack.__exit__(None, None, None)
    return nc


_NEG = np.float32(-1e30)


def _masks():
    qi = np.arange(128)[:, None]
    kk = np.arange(512)[None, :]
    ma = np.concatenate(
        [np.where(kk <= o * 128 + qi, np.float32(0.0), _NEG)
         for o in range(4)], axis=1).astype(ml_dtypes.bfloat16)
    ki = np.arange(128)[:, None]
    qq = np.arange(512)[None, :]
    mb = np.concatenate(
        [np.where(qq >= v * 128 + ki, np.float32(0.0), _NEG)
         for v in range(4)], axis=1).astype(ml_dtypes.bfloat16)
    return ma, mb


def kernel(L0_attn_x, L0_attn_Wq, L0_attn_Wk, L0_attn_Wv, L0_attn_Wo,
           L0_attn_gq, L0_attn_gk, L0_ffn_w1, L0_ffn_w2, L0_ffn_w3):
    f = np.float32
    x = np.asarray(L0_attn_x, f)
    Wq = np.asarray(L0_attn_Wq, f); Wk = np.asarray(L0_attn_Wk, f)
    Wv = np.asarray(L0_attn_Wv, f); Wo = np.asarray(L0_attn_Wo, f)
    gq = np.asarray(L0_attn_gq, f); gk = np.asarray(L0_attn_gk, f)
    w1 = np.asarray(L0_ffn_w1, f); w2 = np.asarray(L0_ffn_w2, f)
    w3 = np.asarray(L0_ffn_w3, f)

    Ct = np.ascontiguousarray
    w1Tm = Ct(w1.T); w2Tm = Ct(w2.T); w3Tm = Ct(w3.T)
    WqT = Ct(Wq.T); WkT = Ct(Wk.T); WvT = Ct(Wv.T); WoT = Ct(Wo.T)
    ma, mb = _masks()
    shared = {
        "w1T": w1Tm, "w2T": w2Tm, "w3T": w3Tm, "woT": WoT,
        "gq": Ct(gq.reshape(HD, 1)), "gk": Ct(gk.reshape(HD, 1)),
        "maskB": mb,
        "ones_b": np.ones((128, 1), ml_dtypes.bfloat16),
        "ones_f": np.ones((128, 1), f),
        "onesr_f": np.ones((1, 128), f),
        "onesr_b": np.ones((1, 128), ml_dtypes.bfloat16),
        "ide": np.eye(128, dtype=f),
        "ide_b": np.eye(128, dtype=ml_dtypes.bfloat16),
    }
    in_maps = []
    for r in range(NCORES):
        b, c = r // 4, r % 4
        m = dict(shared)
        m["xT"] = Ct(x[b, c * TC:(c + 1) * TC, :].T)
        hcs = [h for h in range(r, HC, 8)]
        g1 = np.zeros((D, 768), f); g2 = np.zeros((D, 768), f)
        g3 = np.zeros((768, D), f)
        for i, h in enumerate(hcs):
            g1[:, i * 128:(i + 1) * 128] = w1Tm[:, h * 128:(h + 1) * 128]
            g2[:, i * 128:(i + 1) * 128] = w2Tm[:, h * 128:(h + 1) * 128]
            g3[i * 128:(i + 1) * 128, :] = w3Tm[h * 128:(h + 1) * 128, :]
        m["g1s"] = g1; m["g2s"] = g2; m["g3s"] = g3
        m["wqT"] = Ct(WqT[:, r * 256:(r + 1) * 256])
        m["wkT"] = Ct(WkT[:, r * 256:(r + 1) * 256])
        m["wvT"] = Ct(WvT[:, r * 256:(r + 1) * 256])
        in_maps.append(m)

    nc = build_program()
    res = run_bass_kernel_spmd(nc, in_maps, core_ids=list(range(NCORES)))
    out = np.empty((NB, SEQ, D), f)
    for r in range(NCORES):
        b, c = r // 4, r % 4
        out[b, c * TC:(c + 1) * TC, :] = np.asarray(res.results[r]["y"]).T
    if _TAPS:
        _sys.modules[__name__]._tap_results = res.results
    return out


# version 10
# speedup vs baseline: 1.0026x; 1.0026x over previous
import numpy as np
import ml_dtypes
import sys as _sys
import concourse.bass as bass
import concourse.mybir as mybir
from concourse.bass_utils import run_bass_kernel_spmd

F32 = mybir.dt.float32
BF16 = mybir.dt.bfloat16
AF = mybir.ActivationFunctionType
ALU = mybir.AluOpType
AX = mybir.AxisListType.X

NCORES = 8
D = 2048
HF = 5632
HC = HF // 128
DC = D // 128
TC = 1024
SEQ = 4096
NB = 2
HD = 128
NW = HF * D
EPS_LN = 1e-5
EPS_RMS = 1e-5
EPS_Q = 1e-8
ISQ = 0.08838834764831845

_STAGE = 3
_TAPS = False


class Ring:
    def __init__(self, n):
        self.size = n
        self.rel = [None] * n
        self.i = 0

    def next(self):
        s = self.i % self.size
        self.i += 1
        return s, list(self.rel[s] or [])

    def release(self, s, waits):
        self.rel[s] = list(waits)


class Ctr:
    __slots__ = ("h", "name", "n")


class DmaRing:

    def __init__(self, B, queue, name, depth):
        self.B = B
        self.queue = queue
        self.sems = [B.new_ctr(f"{name}{i}") for i in range(depth)]
        self.rel = [None] * depth
        self.i = 0
        self.depth = depth

    def load(self, fn, waits=()):
        s = self.i % self.depth
        self.i += 1
        w = list(self.rel[s] or []) + list(waits)
        v = self.B.op(self.queue, fn, waits=w, inc=self.sems[s], inck=16)
        return s, [(self.sems[s], v)]

    def load_slot(self, s, fn, waits=()):
        v = self.B.op(self.queue, fn, waits=list(waits),
                      inc=self.sems[s], inck=16)
        return [(self.sems[s], v)]

    def release(self, s, waits):
        self.rel[s] = list(waits)

    def totals(self):
        return [(c, c.n) for c in self.sems]


class Builder:
    ENGS = ("sync", "scalar", "gpsimd", "tensor", "vector")

    def __init__(self, nc, stack):
        self.nc = nc
        self.stack = stack
        self.streams = {e: [] for e in self.ENGS}
        self.ctrs = {}
        for name in ("dW", "dA", "dG", "pe", "dve", "act", "pool", "cc"):
            self.new_ctr(name)

    def new_ctr(self, name):
        c = Ctr.__new__(Ctr)
        c.h = self.stack.enter_context(self.nc.semaphore(name))
        c.name = name
        c.n = 0
        self.ctrs[name] = c
        return c

    def op(self, eng, fn, waits=(), inc=None, inck=1, hwaits=()):
        w = [(c, v) for (c, v) in waits if c is not None and v > 0]
        hw = [(c, v) for (c, v) in hwaits if c is not None and v > 0]
        self.streams[eng].append((w, hw, fn, inc, inck))
        if inc is not None:
            inc.n += inck
            return inc.n
        return None

    def dma(self, queue, fn, waits=()):
        qc = {"sync": "dW", "scalar": "dA", "gpsimd": "dG"}[queue]
        return self.op(queue, fn, waits, inc=self.ctrs[qc], inck=16)

    def dma_on(self, queue, ctr, fn, waits=()):
        return self.op(queue, fn, waits, inc=ctr, inck=16)

    def fence(self):
        snap = [(c, c.n) for c in self.ctrs.values()]
        for eng in self.ENGS:
            self.op(eng, lambda e: None, waits=snap, hwaits=snap)

    def replay(self, eng_name, e):
        own = {"sync": None, "scalar": "act", "gpsimd": "pool",
               "tensor": "pe", "vector": "dve"}[eng_name]
        waited = {}
        for w, hw, fn, inc, inck in self.streams[eng_name]:
            for c, v in w:
                if c.name == own:
                    continue
                if waited.get(c.name, 0) >= v:
                    continue
                e.wait_ge(c.h, v)
                waited[c.name] = v
            for c, v in hw:
                if waited.get(c.name, 0) >= v:
                    continue
                e.wait_ge(c.h, v)
                waited[c.name] = v
            inst = fn(e)
            if inc is not None:
                inst.then_inc(inc.h, inck)


def build_program():
    nc = bass.Bass("TRN2", target_bir_lowering=False, debug=False,
                   detect_race_conditions=False)

    def din(name, shape, dt=F32):
        return nc.dram_tensor(name, shape, dt, kind="ExternalInput")

    xT = din("xT", [D, TC])
    w1T = din("w1T", [D, HF]); w2T = din("w2T", [D, HF]); w3T = din("w3T", [HF, D])
    g1s = din("g1s", [D, 768]); g2s = din("g2s", [D, 768])
    g3s = din("g3s", [768, D])
    wqT = din("wqT", [D, 256]); wkT = din("wkT", [D, 256]); wvT = din("wvT", [D, 256])
    woT = din("woT", [D, D])
    gq_in = din("gq", [HD, 1]); gk_in = din("gk", [HD, 1])
    maskB_in = din("maskB", [128, 2048], BF16)
    ones_b_in = din("ones_b", [128, 1], BF16)
    ones_f_in = din("ones_f", [128, 1])
    onesr_f_in = din("onesr_f", [1, 128])
    onesr_b_in = din("onesr_b", [1, 128], BF16)
    ide_in = din("ide", [128, 128])
    ideb_in = din("ide_b", [128, 128], BF16)

    y = nc.dram_tensor("y", [D, TC], F32, kind="ExternalOutput")
    if _TAPS:
        t_thp = nc.dram_tensor("t_thp", [128, 8], F32, kind="ExternalOutput")
        t_xln = nc.dram_tensor("t_xln", [128, DC * TC], BF16,
                               kind="ExternalOutput")
        t_w1q = nc.dram_tensor("t_w1q", [128, D], BF16, kind="ExternalOutput")
        t_w3q = nc.dram_tensor("t_w3q", [128, D], BF16, kind="ExternalOutput")
        t_hid = nc.dram_tensor("t_hid", [128, 512], BF16, kind="ExternalOutput")
        t_xa = nc.dram_tensor("t_xa", [128, 1024], F32, kind="ExternalOutput")

    agq_in = nc.dram_tensor("agq_in", [256, 3 * D], BF16, kind="Internal")
    agq_in_b = nc.dram_tensor("agq_in_b", [512, 3 * D], BF16, kind="Internal")
    agq_out_a = nc.dram_tensor("agq_out_a", [NCORES * 256, 3 * D], BF16,
                               kind="Internal", addr_space="Shared")
    agq_out_b = nc.dram_tensor("agq_out_b", [NCORES * 512, 3 * D], BF16,
                               kind="Internal", addr_space="Shared")
    agx_in_a = nc.dram_tensor("agx_in_a", [D, 512], BF16, kind="Internal")
    agx_in_b = nc.dram_tensor("agx_in_b", [D, 512], BF16, kind="Internal")
    agx_out_a = nc.dram_tensor("agx_out_a", [NCORES * D, 512], BF16,
                               kind="Internal", addr_space="Shared")
    agx_out_b = nc.dram_tensor("agx_out_b", [NCORES * D, 512], BF16,
                               kind="Internal", addr_space="Shared")
    ago_in = nc.dram_tensor("ago_in", [256, NB * SEQ], BF16, kind="Internal")
    ago_out = nc.dram_tensor("ago_out", [D, NB * SEQ], BF16,
                             kind="Internal", addr_space="Shared")
    woS = nc.dram_tensor("woS", [D, D], BF16, kind="Internal")
    gar_in = nc.dram_tensor("gar_in", [1, 8], F32, kind="Internal")
    gar_out = nc.dram_tensor("gar_out", [1, 8], F32, kind="Internal")

    from contextlib import ExitStack
    stack = ExitStack()
    stack.__enter__()
    B = Builder(nc, stack)
    C = B.ctrs
    dW, dA, dG = C["dW"], C["dA"], C["dG"]
    pe, dve, act, pool, cc = C["pe"], C["dve"], C["act"], C["pool"], C["cc"]

    al = nc.alloc_sbuf_tensor
    X = al("X", [128, DC * TC], F32)
    XLN = al("XLN", [128, DC * TC], BF16)
    MB = al("MB", [128, 2048], BF16)
    IDE = al("IDE", [128, 128], F32)
    IDEB = al("IDEB", [128, 128], BF16)
    ONE_B = al("ONE_B", [128, 1], BF16)
    ONE_F = al("ONE_F", [128, 1], F32)
    ONER_F = al("ONER_F", [1, 128], F32)
    ONER_B = al("ONER_B", [1, 128], BF16)
    GQS = al("GQS", [128, 1], F32); GKS = al("GKS", [128, 1], F32)
    THP = al("THP", [128, 8], F32)
    GSUM = al("GSUM", [128, 4], F32)
    GROW = al("GROW", [1, 16], F32)
    COLS = al("COLS", [128, 24], F32)
    DIAG = al("DIAG", [128, 128], F32)
    TMPC = al("TMPC", [128, 1], F32)

    ARENA_BYTES = 88 * 1024
    arena = al("ARENA", [128, ARENA_BYTES], mybir.dt.uint8)
    abase = nc.lookup_mloc(arena).addr
    _aid = [0]

    def at(shape, dt, off):
        _aid[0] += 1
        return nc.alloc_sbuf_tensor_at(f"ar{_aid[0]}", shape, dt,
                                       offset=abase + off)

    STG = at([128, 2, 2048], F32, 0)
    U2S = at([128, 2048], BF16, 16384)
    W1Q = at([128, 2, 2048], BF16, 20480)
    W2Q = at([128, 2, 2048], BF16, 28672)
    W3Q = at([128, 2, 1024], BF16, 36864)
    HID = at([128, HC, 512], BF16, 40960)
    SG = at([128, 2, 512], BF16, 86016)

    TMPF = al("TMPF", [128, 512], F32)
    X2S = al("X2S", [128, 1024], BF16)
    MBR = al("MBR", [128, 1024], BF16)
    RBR = al("RBR", [128, 1024], BF16)


    GST = STG

    QA = at([128, SEQ], BF16, 0)
    QB = at([128, SEQ], BF16, 8192)
    KA = at([128, SEQ], BF16, 16384)
    KB = at([128, SEQ], BF16, 24576)
    VV = at([128, 2 * SEQ], BF16, 32768)
    XGR = at([128, 6, 512], BF16, 49152)
    PBR = at([128, 2, 1024], BF16, 55296)
    PSB = at([128, 2, 512], BF16, 59392)
    Q2S = at([128, 2, 512], BF16, 61440)
    OSTB = at([128, 2, 512], BF16, 63488)
    LBS = at([128, 512], F32, 65536)
    LROW = at([1, 512], F32, 67584)
    LRB = at([1, 512], BF16, 69632)
    R0 = at([1, 1024], F32, 70656)
    R1 = at([1, 1024], F32, 74752)
    RSR = at([1, 512], F32, 78848)
    RSB = at([1, 512], BF16, 80896)
    XBH = at([128, 2, 512], BF16, 61440)
    R2 = at([1, 1024], F32, 81920)

    xlnb = nc.lookup_mloc(XLN).addr
    WQ = nc.alloc_sbuf_tensor_at("WQ", [128, DC, 256], BF16, offset=xlnb)
    WK = nc.alloc_sbuf_tensor_at("WK", [128, DC, 256], BF16, offset=xlnb + 8192)
    WV = nc.alloc_sbuf_tensor_at("WV", [128, DC, 256], BF16,
                                 offset=xlnb + 16384)

    OM = at([128, DC, TC], BF16, 0)
    WOS = at([128, 4, 2048], BF16, 32768)

    PS1 = nc.alloc_psum_tensor("PS1", [128, 2048], F32)
    PS2 = nc.alloc_psum_tensor("PS2", [128, 2048], F32)

    def b1(i):
        return PS1[:, 512 * i:512 * (i + 1)]

    def b2(i):
        return PS2[:, 512 * i:512 * (i + 1)]

    def MM(out_ap, lhsT_ap, rhs_ap, start, stop, waits=(), inc=None):
        return B.op("tensor",
                    lambda e, o=out_ap, l=lhsT_ap, r=rhs_ap, s=start, t=stop:
                    e.matmul(o, l, r, start=s, stop=t), waits, inc=inc)

    def xsl(dc, tq):
        return X[:, dc * TC + tq * 512: dc * TC + (tq + 1) * 512]

    def xlsl(dc, tq):
        return XLN[:, dc * TC + tq * 512: dc * TC + (tq + 1) * 512]

    dx = B.new_ctr("dx")
    for dc in range(DC):
        B.dma_on("sync", dx, lambda e, dc=dc: e.dma_start(
            out=X[:, dc * TC:(dc + 1) * TC], in_=xT[dc * 128:(dc + 1) * 128, :]))
    x_loaded = dx.n
    xw = (dx, x_loaded)

    dcst = B.new_ctr("dcst")
    for dst, src in ((MB, maskB_in), (IDE, ide_in),
                     (IDEB, ideb_in), (ONER_B, onesr_b_in),
                     (ONE_B, ones_b_in), (ONE_F, ones_f_in),
                     (ONER_F, onesr_f_in), (GQS, gq_in), (GKS, gk_in)):
        B.dma_on("scalar", dcst, lambda e, d=dst, s=src:
                 e.dma_start(out=d[:], in_=s[:]))
    const_loaded = dcst.n
    cw = (dcst, const_loaded)
    B.op("vector", lambda e: e.tensor_scalar_mul(GQS[:], GQS[:], ISQ),
         waits=[cw], inc=dve)
    B.op("vector", lambda e: e.memset(GROW[:], 0.0), inc=dve)
    B.op("vector", lambda e: e.memset(THP[:], 0.0), inc=dve)
    B.op("vector", lambda e: e.memset(THP[:, 3:4], -0.5), inc=dve)
    B.op("vector", lambda e: e.memset(THP[:, 7:8], -20.0), inc=dve)

    def emit_gamma():
        ring = DmaRing(B, "sync", "gst", 2)
        dgar = B.new_ctr("dgar")
        plan = []
        for j, (g, rows, cols) in enumerate(
                [(g1s, D, 768), (g2s, D, 768), (g3s, 768, D)]):
            per = max(1, 2048 // cols)
            r0 = 0
            while r0 < rows // 128:
                n = min(per, rows // 128 - r0)
                plan.append((g, r0, n, cols, j))
                r0 += n
        jcols = {0: [], 1: [], 2: []}
        for ci, (g, r0, n, cols, j) in enumerate(plan):
            def ldfn(e, g=g, r0=r0, n=n, cols=cols, s=None):
                return e.dma_start(
                    out=GST[:, s, 0:n * cols].rearrange("p (a h) -> p a h", a=n),
                    in_=g[r0 * 128:(r0 + n) * 128, :].rearrange(
                        "(a p) h -> p a h", p=128))
            slot = ring.i % ring.depth
            _, ldw = ring.load(
                lambda e, f=ldfn, s=slot: f(e, s=s))
            rd = B.op("vector", lambda e, s=slot, n=n, cols=cols, ci=ci:
                      e.reduce_sum(COLS[:, ci:ci + 1], GST[:, s, 0:n * cols],
                                   AX, apply_absolute_value=True),
                      waits=ldw, inc=dve)
            jcols[j].append(ci)
            ring.release(slot, [(dve, rd)])
        allred = dve.n
        for j in range(3):
            c0, c1 = jcols[j][0], jcols[j][-1] + 1
            gs = B.op("vector", lambda e, j=j, c0=c0, c1=c1: e.reduce_sum(
                GSUM[:, j:j + 1], COLS[:, c0:c1], AX),
                hwaits=[(dve, allred)], inc=dve)
            dsn = B.op("vector", lambda e, j=j: e.tensor_scalar_mul(
                DIAG[:], IDE[:], GSUM[:, j:j + 1]),
                waits=[cw, (pe, pe.n)],
                hwaits=[(dve, gs)], inc=dve)
            msn = MM(PS1[0:1, 128 * j:128 * j + 128], ONE_F[:], DIAG[:],
                     True, True, waits=[(dve, dsn)], inc=pe)
            B.op("vector", lambda e, j=j: e.reduce_sum(
                GROW[0:1, j:j + 1], PS1[0:1, 128 * j:128 * j + 128], AX),
                waits=[(pe, msn)], inc=dve)
        gsn = dve.n
        s1 = B.dma_on("gpsimd", dgar, lambda e: e.dma_start(
            out=gar_in[:, :], in_=GROW[0:1, 0:8]), waits=[(dve, gsn)])
        c1 = B.op("gpsimd", lambda e: e.collective_compute(
            "AllReduce", ALU.add, replica_groups=[list(range(NCORES))],
            ins=[gar_in[:, :].opt()], outs=[gar_out[:, :].opt()]),
            waits=[(dgar, s1)], inc=cc)
        s2 = B.dma_on("gpsimd", dgar, lambda e: e.dma_start(
            out=GROW[0:1, 8:16], in_=gar_out[:, :]), waits=[(cc, c1)])
        tsn = B.op("vector", lambda e: e.tensor_scalar(
            GROW[0:1, 8:16], GROW[0:1, 8:16], 0.5 / NW, 0.5 * EPS_Q,
            ALU.mult, ALU.add), waits=[(dgar, s2)], inc=dve)
        for j in range(3):
            msn = MM(PS1[:, 512 + j:513 + j], ONER_F[:], GROW[0:1, 8 + j:9 + j],
                     True, True, waits=[(dve, tsn)], inc=pe)
            B.op("vector", lambda e, j=j: e.tensor_copy(
                THP[:, j:j + 1], PS1[:, 512 + j:513 + j]),
                waits=[(pe, msn)], inc=dve)
            B.op("vector", lambda e, j=j: e.tensor_scalar_mul(
                THP[:, 4 + j:5 + j], PS1[:, 512 + j:513 + j], -1.0), inc=dve)
        return dve.n

    gamma_done = emit_gamma()

    def emit_layernorm(prereq, tqs=(0, 1), alt=False):
        xbh = XBH2 if alt else XBH
        prev_mm = {}
        cnt = 0
        for dc in range(DC):
            for tq in tqs:
                i = cnt
                cnt += 1
                cs = B.op("scalar", lambda e, dc=dc, tq=tq, i=i: e.activation(
                    xbh[:, i % 2, :], xsl(dc, tq), AF.Identity),
                    waits=list(prereq) + [(pe, prev_mm.get(i - 2, 0))],
                    inc=act)
                sq = B.op("scalar", lambda e, dc=dc, tq=tq, i=i: e.activation(
                    X2S[:, (i % 2) * 512:(i % 2 + 1) * 512], xsl(dc, tq),
                    AF.Square), inc=act)
                MM(PS2[0:1, 1024 + tq * 512:1536 + tq * 512], ONE_B[:],
                   xbh[:, i % 2, :], start=(dc == 0), stop=(dc == DC - 1),
                   waits=[(act, cs)])
                m = MM(PS1[0:1, 1024 + tq * 512:1536 + tq * 512], ONE_B[:],
                       X2S[:, (i % 2) * 512:(i % 2 + 1) * 512],
                       start=(dc == 0), stop=(dc == DC - 1),
                       waits=[(act, sq)], inc=pe)
                prev_mm[i] = m
        stat_done = pe.n
        for tq in tqs:
            if alt:
                r0, r1, r2 = R0b[0:1, :], R1b[0:1, :], R2b[0:1, :]
            else:
                r0 = R0[0:1, tq * 512:(tq + 1) * 512]
                r1 = R1[0:1, tq * 512:(tq + 1) * 512]
                r2 = R2[0:1, tq * 512:(tq + 1) * 512]
            s1 = B.op("vector", lambda e, r0=r0, tq=tq: e.tensor_scalar_mul(
                r0, PS2[0:1, 1024 + tq * 512:1536 + tq * 512], 1.0 / D),
                waits=[(pe, stat_done)], inc=dve)
            s2 = B.op("vector", lambda e, r1=r1, tq=tq: e.tensor_scalar_mul(
                r1, PS1[0:1, 1024 + tq * 512:1536 + tq * 512], 1.0 / D),
                inc=dve)
            s3 = B.op("vector", lambda e, r0=r0, r2=r2: e.tensor_mul(
                r2, r0, r0), hwaits=[(dve, s1)], inc=dve)
            s4 = B.op("vector", lambda e, r1=r1, r2=r2: e.tensor_sub(
                r1, r1, r2), hwaits=[(dve, s3)], inc=dve)
            vsn = B.op("vector", lambda e, r1=r1: e.tensor_scalar_add(
                r1, r1, EPS_LN), hwaits=[(dve, s4)], inc=dve)
            ssn = B.op("scalar", lambda e, r1=r1: e.activation(
                r1, r1, AF.Ln), waits=[(dve, vsn)], inc=act)
            rsn2 = B.op("scalar", lambda e, r1=r1: e.activation(
                r1, r1, AF.Exp, scale=-0.5), inc=act)
            rsn = B.op("vector", lambda e: e.engine_nop(),
                       waits=[(act, rsn2)], inc=dve)
            m1 = MM(b2(tq), ONER_F[:], r0, True, True,
                    waits=[(dve, rsn)], inc=pe)
            B.op("vector", lambda e, tq=tq: e.tensor_copy(
                MBR[:, tq * 512:(tq + 1) * 512], b2(tq)),
                waits=[(pe, m1)], inc=dve)
            m2 = MM(b2(2 + tq), ONER_F[:], r1, True, True, inc=pe)
            B.op("vector", lambda e, tq=tq: e.tensor_copy(
                RBR[:, tq * 512:(tq + 1) * 512], b2(2 + tq)),
                waits=[(pe, m2)], inc=dve)
            bc = dve.n
            for dc in range(DC):
                B.op("vector", lambda e, dc=dc, tq=tq: e.tensor_sub(
                    TMPF[:, :], xsl(dc, tq),
                    MBR[:, tq * 512:(tq + 1) * 512]),
                    waits=[(dve, bc)], inc=dve)
                B.op("vector", lambda e, dc=dc, tq=tq: e.tensor_mul(
                    xlsl(dc, tq), TMPF[:, :],
                    RBR[:, tq * 512:(tq + 1) * 512]), inc=dve)
        return dve.n

    stg = Ring(2)
    stgd = DmaRing(B, "sync", "stg", 2)
    qsc = DmaRing(B, "scalar", "qsc", 2)
    agq_ready = [0, 0]
    qsc_early = [[]]

    QOBT = al("QOBT", [128, 2, 2048], BF16)
    qobase = nc.lookup_mloc(QOBT).addr
    R0b = nc.alloc_sbuf_tensor_at("R0b", [1, 512], F32, offset=qobase)
    R1b = nc.alloc_sbuf_tensor_at("R1b", [1, 512], F32, offset=qobase + 2048)
    R2b = nc.alloc_sbuf_tensor_at("R2b", [1, 512], F32, offset=qobase + 4096)
    XBH2 = nc.alloc_sbuf_tensor_at("XBH2", [128, 2, 512], BF16,
                                   offset=qobase + 6144)



    def emit_qshard():
        QOB = [QOBT[:, 0, :], QOBT[:, 1, :]]
        SGN = U2S[:, :]
        prev_q = [[], []]
        for u in range(6):
            for j, g in enumerate((g1s, g2s, g3s)):
                ss = stgd.i % stgd.depth
                if j == 2:
                    _, ldw = stgd.load(lambda e, u=u, s=ss: e.dma_start(
                        out=STG[:, s, :],
                        in_=g3s[u * 128:(u + 1) * 128, :]),
                        waits=[(dve, gamma_done)])
                else:
                    _, ldw = stgd.load(lambda e, g=g, u=u, s=ss: e.dma_start(
                        out=STG[:, s, :].rearrange("p (a h) -> p a h", a=16),
                        in_=g[:, u * 128:(u + 1) * 128].rearrange(
                            "(a p) h -> p a h", p=128)),
                        waits=[(dve, gamma_done)])
                qi = (u * 3 + j) % 2
                u1 = B.op("vector", lambda e, s=ss, j=j, qi=qi:
                          e.tensor_scalar(
                    QOB[qi], STG[:, s, :], THP[:, j:j + 1], THP[:, 3:4],
                    ALU.is_ge, ALU.add),
                    waits=ldw + [(dve, gamma_done)] + prev_q[qi], inc=dve)
                sg = B.op("scalar", lambda e, s=ss, j=j: e.activation(
                    SGN, STG[:, s, :], AF.Sign, bias=THP[:, j:j + 1]),
                    waits=ldw + [(dve, _qd_used[0])], inc=act)
                qd = B.op("vector", lambda e, qi=qi: e.scalar_tensor_tensor(
                    QOB[qi], SGN, 0.5, QOB[qi], ALU.mult, ALU.add),
                    waits=[(act, sg)], inc=dve)
                _qd_used[0] = qd
                st = qsc.load_slot(qi, lambda e, qi=qi, u=u, j=j: e.dma_start(
                    out=(agq_in[u * 128:(u + 1) * 128, j * D:(j + 1) * D]
                         if u < 2 else
                         agq_in_b[(u - 2) * 128:(u - 1) * 128,
                                  j * D:(j + 1) * D]),
                    in_=QOB[qi]), waits=[(dve, qd)])
                prev_q[qi] = st
                stgd.release(ss, [(dve, u1), (act, sg)])
            if u == 1:
                qsc_early[0] = qsc.totals()
        aga = B.op("gpsimd", lambda e: e.collective_compute(
            "AllGather", ALU.bypass, replica_groups=[list(range(NCORES))],
            ins=[agq_in[:, :].opt()], outs=[agq_out_a[:, :].opt()]),
            waits=qsc_early[0], inc=cc)
        agb = B.op("gpsimd", lambda e: e.collective_compute(
            "AllGather", ALU.bypass, replica_groups=[list(range(NCORES))],
            ins=[agq_in_b[:, :].opt()], outs=[agq_out_b[:, :].opt()]),
            waits=qsc.totals(), inc=cc)
        agq_ready[0] = aga
        agq_ready[1] = agb

    _qd_used = [0]
    _sgn_used = [0]

    r1 = Ring(2)
    r2 = Ring(2)
    w1l = DmaRing(B, "sync", "w1l", 2)
    w2l = DmaRing(B, "sync", "w2l", 2)
    w3l = DmaRing(B, "sync", "w3l", 2)
    r3 = Ring(2)
    p2evac = [0]

    def qsrc(hc):
        if hc < 16:
            return agq_out_a, (hc % 8) * 256 + (hc // 8) * 128
        return agq_out_b, (hc % 8) * 512 + (hc // 8 - 2) * 128

    def emit_ffn(first, ln_done, mid_hook=None):
        sig_done = {}
        gate_done = {}
        ln_gate = [ln_done]
        evacs_prev = [p2evac[0]]
        for tq in range(2):
            sig_done.clear()
            gate_done.clear()
            for hc in range(HC):
                consume = {}
                for widx, wqbuf, wring, ldr in (
                        (0, W1Q, r1, w1l), (1, W2Q, r2, w2l)):
                    qs, qw = wring.next()
                    ldw = ldr.load_slot(qs, lambda e, hc=hc, q=qs, wq=wqbuf,
                                        widx=widx: e.dma_start(
                        out=wq[:, q, :],
                        in_=qsrc(hc)[0][qsrc(hc)[1]:qsrc(hc)[1] + 128,
                                        widx * D:(widx + 1) * D]),
                        waits=list(qw) + [(cc, agq_ready[0 if hc < 16 else 1])])
                    consume[widx] = (qs, ldw)
                xb = hc % 2
                q1, w1w = consume[0]
                q2, w2w = consume[1]
                for dc in range(DC):
                    MM(b1(xb), W1Q[:, q1, dc * 128:(dc + 1) * 128],
                       xlsl(dc, tq), start=(dc == 0), stop=(dc == DC - 1),
                       waits=(w1w + [(dve, ln_gate[0]),
                                     (act, sig_done.get(hc - 2, 0)),
                                     (dve, evacs_prev[0])])
                       if dc == 0 else (),
                       inc=(pe if dc == DC - 1 else None))
                x1g = pe.n
                for dc in range(DC):
                    MM(b1(2 + xb), W2Q[:, q2, dc * 128:(dc + 1) * 128],
                       xlsl(dc, tq), start=(dc == 0), stop=(dc == DC - 1),
                       waits=(w2w + [(dve, gate_done.get(hc - 2, 0)),
                                     (dve, evacs_prev[0])])
                       if dc == 0 else (),
                       inc=(pe if dc == DC - 1 else None))
                x2g = pe.n
                r1.release(q1, [(pe, x1g)])
                r2.release(q2, [(pe, x2g)])
                sg = B.op("scalar", lambda e, xb=xb: e.activation(
                    SG[:, xb, :], b1(xb), AF.Sigmoid),
                    waits=[(pe, x1g), (dve, gate_done.get(hc - 2, 0))],
                    inc=act)
                hd_ = B.op("vector", lambda e, xb=xb, hc=hc: e.tensor_tensor(
                    HID[:, hc, :], b1(2 + xb), SG[:, xb, :], ALU.mult),
                    waits=[(pe, x2g), (act, sg)], inc=dve)
                sig_done[hc] = sg
                gate_done[hc] = hd_
            p1_dve = dve.n
            for dtb in range(2):
                grp = 0
                for hc in range(HC):
                    s3, w3w_ = r3.next()
                    ldw = w3l.load_slot(s3, lambda e, hc=hc, dtb=dtb, s=s3:
                                        e.dma_start(
                        out=W3Q[:, s, :],
                        in_=qsrc(hc)[0][qsrc(hc)[1]:qsrc(hc)[1] + 128,
                                        2 * D + dtb * 1024:
                                        2 * D + (dtb + 1) * 1024]),
                        waits=list(w3w_) + [(cc, agq_ready[0 if hc < 16 else 1])])
                    for dtl in range(8):
                        ps = b1(dtl) if dtl < 4 else b2(dtl - 4)
                        grp = MM(ps, W3Q[:, s3, dtl * 128:(dtl + 1) * 128],
                                 HID[:, hc, :],
                                 start=(hc == 0), stop=(hc == HC - 1),
                                 waits=(ldw + [(dve, p1_dve),
                                               (dve, evacs_prev[0])])
                                 if dtl == 0 else ldw,
                                 inc=(pe if dtl == 7 else None))
                    r3.release(s3, [(pe, grp)])
                for dtl in range(8):
                    dt = dtb * 8 + dtl
                    ps = b1(dtl) if dtl < 4 else b2(dtl - 4)
                    B.op("vector", lambda e, dt=dt, ps=ps, tq=tq:
                         e.scalar_tensor_tensor(
                        xsl(dt, tq), ps, 0.5, xsl(dt, tq), ALU.mult, ALU.add),
                        waits=[(pe, grp)], inc=dve)
                evacs_prev[0] = dve.n
            if tq == 0 and mid_hook is not None:
                r = mid_hook(dve.n)
                if r is not None:
                    ln_gate[0] = r
        p2evac[0] = dve.n
        return dve.n

    dagx = B.new_ctr("dagx")
    ag1cc = [0, 0]

    def emit_ag1_half(tq, ln_done):
        gin = (agx_in_a, agx_in_b)[tq]
        gout = (agx_out_a, agx_out_b)[tq]
        for dc in range(DC):
            B.dma_on("scalar", dagx, lambda e, dc=dc, tq=tq, gin=gin:
                     e.dma_start(
                out=gin[dc * 128:(dc + 1) * 128, :],
                in_=xlsl(dc, tq)), waits=[(dve, ln_done)])
        ag = B.op("gpsimd", lambda e, gin=gin, gout=gout:
                  e.collective_compute(
            "AllGather", ALU.bypass, replica_groups=[list(range(NCORES))],
            ins=[gin[:, :].opt()], outs=[gout[:, :].opt()]),
            waits=[(dagx, dagx.n)], inc=cc)
        ag1cc[tq] = ag

    def emit_attention():
        contrib = dagx.n
        dwqkv = B.new_ctr("dwqkv")
        for w_in, wbuf in ((wqT, WQ), (wkT, WK), (wvT, WV)):
            B.dma_on("gpsimd", dwqkv, lambda e, w_in=w_in, wbuf=wbuf:
                     e.dma_start(
                out=wbuf[:], in_=w_in.rearrange("(a p) c -> p a c", p=128)),
                waits=[(dagx, contrib)])
        B.dma_on("gpsimd", dwqkv, lambda e: e.dma_start(
            out=woS[:, :], in_=woT[:, :]), waits=[(dagx, contrib)])
        wload = (dwqkv, dwqkv.n)

        xg = DmaRing(B, "scalar", "xg", 6)
        ostw = DmaRing(B, "gpsimd", "ostw", 2)
        scr_ring = Ring(3)
        pb_ring = Ring(2)
        ost_ring = Ring(2)
        passA_credit = [None, None]
        passB_credit = [None, None, None]
        qk_ready = [0]
        v_ready = [0]
        prev_rc = [0]
        prev_bc = [0]
        nmr_used = [0]
        lbs_used = [0]
        lrow_done = [0]
        ov_done = [0]
        pb_prev = [None]
        pair_cnt = [0]
        pair_credit = [None, None]
        psb_i = [0]
        psb_rel = [None, None]

        def emit_proj(b):
            B.fence()
            prev_evac = {}
            vprev = 0
            for tq8 in range(8):
                vlast = 0
                for dc in range(DC):
                    s = xg.i % xg.depth
                    gout = (agx_out_a, agx_out_b)[tq8 % 2]
                    _, ldw = xg.load(lambda e, b=b, tq8=tq8, dc=dc, s=s,
                                     gout=gout: e.dma_start(
                        out=XGR[:, s, :],
                        in_=gout[(4 * b + tq8 // 2) * D + dc * 128:
                                 (4 * b + tq8 // 2) * D + (dc + 1) * 128, :]),
                        waits=[(cc, ag1cc[tq8 % 2])])
                    w0 = ldw + [wload]
                    if dc == 0:
                        for t in range(4):
                            w0 += prev_evac.get(t, [])
                        if vprev:
                            w0 += [(act, vprev)]
                    MM(b1(0), WQ[:, dc, 0:128], XGR[:, s, :],
                       start=(dc == 0), stop=(dc == DC - 1), waits=w0)
                    MM(b1(1), WQ[:, dc, 128:256], XGR[:, s, :],
                       start=(dc == 0), stop=(dc == DC - 1))
                    MM(b1(2), WK[:, dc, 0:128], XGR[:, s, :],
                       start=(dc == 0), stop=(dc == DC - 1))
                    MM(b1(3), WK[:, dc, 128:256], XGR[:, s, :],
                       start=(dc == 0), stop=(dc == DC - 1))
                    for t4 in range(4):
                        vlast = MM(PS2[:, t4 * 256:(t4 + 1) * 256],
                                   XGR[:, s, t4 * 128:(t4 + 1) * 128],
                                   WV[:, dc, :],
                                   start=(dc == 0), stop=(dc == DC - 1),
                                   inc=(pe if t4 == 3 else None))
                    xg.release(s, [(pe, vlast)])
                grp = vlast
                prev_rw = 0
                prev_ap = 0
                prev_rsmm = {}
                for t, (psb, dbuf, gcol) in enumerate(
                        ((b1(0), QA, GQS), (b1(1), QB, GQS),
                         (b1(2), KA, GKS), (b1(3), KB, GKS))):
                    sq = B.op("scalar", lambda e, psb=psb, t=t: e.activation(
                        Q2S[:, t % 2, :], psb, AF.Square),
                        waits=[(pe, grp), (pe, prev_rsmm.get(t - 2, 0))],
                        inc=act)
                    cp = B.op("scalar", lambda e, psb=psb, dbuf=dbuf, tq8=tq8:
                              e.activation(
                        dbuf[:, tq8 * 512:(tq8 + 1) * 512], psb, AF.Identity),
                        inc=act)
                    mm = MM(PS2[0:1, 1024:1536], ONE_B[:], Q2S[:, t % 2, :],
                            True, True,
                            waits=[(act, sq), (dve, prev_rw)], inc=pe)
                    prev_rsmm[t] = mm
                    rw = B.op("vector", lambda e: e.tensor_scalar(
                        RSR[0:1, :], PS2[0:1, 1024:1536],
                        1.0 / HD, EPS_RMS, ALU.mult, ALU.add),
                        waits=[(pe, mm), (act, prev_rc[0])], inc=dve)
                    sr = B.op("scalar", lambda e: e.activation(
                        RSR[0:1, :], RSR[0:1, :], AF.Ln),
                        waits=[(dve, rw)], inc=act)
                    rc = B.op("scalar", lambda e: e.activation(
                        RSB[0:1, :], RSR[0:1, :], AF.Exp, scale=-0.5),
                        waits=[(pe, prev_bc[0])], inc=act)
                    prev_rc[0] = rc
                    bc = MM(PS2[:, 1536:2048], ONER_B[:],
                            RSB[0:1, :], True, True,
                            waits=[(act, rc), (dve, prev_ap)], inc=pe)
                    prev_bc[0] = bc
                    ap = B.op("vector", lambda e, dbuf=dbuf, tq8=tq8, gcol=gcol:
                              e.scalar_tensor_tensor(
                        dbuf[:, tq8 * 512:(tq8 + 1) * 512],
                        dbuf[:, tq8 * 512:(tq8 + 1) * 512], gcol[:],
                        PS2[:, 1536:2048], ALU.mult, ALU.mult),
                        waits=[(pe, bc), (act, cp)], inc=dve)
                    prev_rw = rw
                    prev_ap = ap
                    prev_evac[t] = [(dve, ap), (act, cp)]
                for t4 in range(4):
                    vprev = B.op("scalar", lambda e, tq8=tq8, t4=t4:
                                 e.activation(
                        VV[:, (tq8 * 4 + t4) * 256:(tq8 * 4 + t4 + 1) * 256],
                        PS2[:, t4 * 256:(t4 + 1) * 256], AF.Identity),
                        waits=[(pe, grp)], inc=act)
            qk_ready[0] = dve.n
            v_ready[0] = act.n

        def emit_passB_qg(b, h, qg):
            Q = (QA, QB)[h]
            K = (KA, KB)[h]
            nkc = 4 * qg + 4
            pend = None
            grp = 0
            for pr in range(nkc // 2):
                half = pair_cnt[0] % 2
                pair_cnt[0] += 1
                hw0 = [(dve, qk_ready[0])] \
                    + ([pair_credit[half]] if pair_credit[half] else [])
                for sub in range(2):
                    kc = 2 * pr + sub
                    bank = b2(half * 2 + sub)
                    w = hw0 if sub == 0 else ()
                    if kc >= 4 * qg:
                        v = kc - 4 * qg
                        MM(bank, IDEB[:], MB[:, v * 512:(v + 1) * 512],
                           True, False, waits=w)
                        mm = MM(bank, K[:, kc * 128:(kc + 1) * 128],
                                Q[:, qg * 512:(qg + 1) * 512], False, True,
                                inc=(pe if sub == 1 else None))
                    else:
                        mm = MM(bank, K[:, kc * 128:(kc + 1) * 128],
                                Q[:, qg * 512:(qg + 1) * 512], True, True,
                                waits=w, inc=(pe if sub == 1 else None))
                if pb_prev[0] is not None:
                    pb_ring.release(pb_prev[0], [(pe, mm)])
                ps_, pw = pb_ring.next()
                ex = B.op("scalar", lambda e, half=half, p=ps_: e.activation(
                    PBR[:, p, :], PS2[:, half * 1024:(half + 1) * 1024],
                    AF.Exp, bias=THP[:, 7:8]),
                    waits=[(pe, mm)] + pw, inc=act)
                pair_credit[half] = (act, ex)
                sb = psb_i[0] % 2
                psb_i[0] += 1
                sm = B.op("vector", lambda e, p=ps_, sb=sb: e.tensor_add(
                    PSB[:, sb, :], PBR[:, p, 0:512], PBR[:, p, 512:1024]),
                    waits=[(act, ex)] + (psb_rel[sb] or []), inc=dve)
                if pend is not None:
                    pps, ppr, psb_, psm = pend
                    MM(PS1[0:1, 1536:2048], ONE_B[:], PSB[:, psb_, :],
                       start=(ppr == 0), stop=False,
                       waits=[(dve, psm)]
                       + ([(dve, lbs_used[0])] if ppr == 0 else []))
                    psb_rel[psb_] = [(pe, pe.n)]
                    for s2 in range(2):
                        pkc = 2 * ppr + s2
                        MM(PS1[:, 1024:1536],
                           VV[:, pkc * 256 + h * 128: pkc * 256 + h * 128 + 128],
                           PBR[:, pps, s2 * 512:(s2 + 1) * 512],
                           start=(pkc == 0), stop=False,
                           waits=([(dve, ov_done[0]), (act, v_ready[0])]
                                  if pkc == 0 else ()))
                pend = (ps_, pr, sb, sm)
            pps, ppr, psb_, psm = pend
            MM(PS1[0:1, 1536:2048], ONE_B[:], PSB[:, psb_, :],
               start=(ppr == 0), stop=True,
               waits=[(dve, psm)] + ([(dve, lbs_used[0])] if ppr == 0 else []))
            psb_rel[psb_] = [(pe, pe.n)]
            for s2 in range(2):
                pkc = 2 * ppr + s2
                grp = MM(PS1[:, 1024:1536],
                         VV[:, pkc * 256 + h * 128: pkc * 256 + h * 128 + 128],
                         PBR[:, pps, s2 * 512:(s2 + 1) * 512],
                         start=(pkc == 0), stop=(s2 == 1),
                         waits=([(dve, ov_done[0]), (act, v_ready[0])]
                                if pkc == 0 else ()),
                         inc=(pe if s2 == 1 else None))
            pb_ring.release(pps, [(pe, grp)])
            pb_prev[0] = None
            lr = B.op("vector", lambda e: e.tensor_scalar_add(
                LROW[0:1, :], PS1[0:1, 1536:2048], 1e-38),
                waits=[(pe, grp)], inc=dve)
            lrow_done[0] = lr
            l1 = B.op("scalar", lambda e: e.activation(
                LROW[0:1, :], LROW[0:1, :], AF.Ln), waits=[(dve, lr)], inc=act)
            rc = B.op("scalar", lambda e: e.activation(
                LRB[0:1, :], LROW[0:1, :], AF.Exp, scale=-1.0), inc=act)
            lb = MM(PS1[:, 1536:2048], ONER_B[:], LRB[0:1, :], True, True,
                    waits=[(act, rc), (dve, lbs_used[0]), (dve, lr)], inc=pe)
            lv = B.op("vector", lambda e: e.tensor_copy(
                LBS[:, :], PS1[:, 1536:2048]), waits=[(pe, lb)], inc=dve)
            lbs_used[0] = lv
            os_, ow = ost_ring.next()
            ov = B.op("vector", lambda e, o=os_: e.tensor_tensor(
                OSTB[:, o, :], PS1[:, 1024:1536], LBS[:, :], ALU.mult),
                waits=[(pe, grp)] + ow, inc=dve)
            ov_done[0] = ov
            odw = ostw.load_slot(os_, lambda e, o=os_, b=b, h=h, qg=qg:
                                 e.dma_start(
                out=ago_in[h * 128:(h + 1) * 128,
                           b * SEQ + qg * 512: b * SEQ + (qg + 1) * 512],
                in_=OSTB[:, o, :]), waits=[(dve, ov)])
            ost_ring.release(os_, odw)

        for b in range(NB):
            emit_proj(b)
            for h in range(2):
                for qg in range(8):
                    emit_passB_qg(b, h, qg)

        ag2 = B.op("gpsimd", lambda e: e.collective_compute(
            "AllGather", ALU.bypass, replica_groups=[list(range(NCORES))],
            ins=[ago_in[:, :].opt()], outs=[ago_out[:, :].opt()]),
            waits=ostw.totals(), inc=cc)
        B.fence()
        pid_cell = []
        dom = B.new_ctr("dom")

        def om_read(e, dv):
            if not pid_cell:
                pid_cell.append(e.partition_id())
            return e.dma_start(
                out=OM[:, dv, :],
                in_=ago_out[dv * 128:(dv + 1) * 128, bass.ts(pid_cell[0], TC)])

        for dv in range(DC):
            B.dma_on("gpsimd", dom, lambda e, dv=dv: om_read(e, dv),
                     waits=[(cc, ag2)])
        om_done = (dom, dom.n)
        wos = DmaRing(B, "sync", "wos", 4)
        wo_evac = [p2evac[0]]
        for dog in range(4):
            grp = 0
            for dv in range(DC):
                s = wos.i % wos.depth
                _, ldw = wos.load(lambda e, dv=dv, s=s: e.dma_start(
                    out=WOS[:, s, :], in_=woS[dv * 128:(dv + 1) * 128, :]),
                    waits=[om_done, wload])
                for dol in range(4):
                    do = dog * 4 + dol
                    for tq in range(2):
                        bk = dol * 2 + tq
                        ps = b1(bk) if bk < 4 else b2(bk - 4)
                        grp = MM(ps, WOS[:, s, do * 128:(do + 1) * 128],
                                 OM[:, dv, tq * 512:(tq + 1) * 512],
                                 start=(dv == 0), stop=(dv == DC - 1),
                                 waits=(ldw + [(dve, wo_evac[0])]
                                        if (dol == 0 and tq == 0) else ldw),
                                 inc=(pe if (dol == 3 and tq == 1) else None))
                wos.release(s, [(pe, grp)])
            for dol in range(4):
                do = dog * 4 + dol
                for tq in range(2):
                    bk = dol * 2 + tq
                    ps = b1(bk) if bk < 4 else b2(bk - 4)
                    B.op("vector", lambda e, do=do, tq=tq, ps=ps:
                         e.tensor_tensor(
                        xsl(do, tq), ps, xsl(do, tq), ALU.add),
                        waits=[(pe, grp)], inc=dve)
            wo_evac[0] = dve.n
        p2evac[0] = dve.n
        return dve.n

    emit_qshard()
    ln1 = emit_layernorm([xw])
    if _TAPS:
        B.dma("gpsimd", lambda e: e.dma_start(out=t_thp[:, :], in_=THP[:]),
              waits=[(dve, gamma_done)])
        B.dma("gpsimd", lambda e: e.dma_start(out=t_xln[:, :], in_=XLN[:]),
              waits=[(dve, ln1)])
    def _ln2a_hook(tq0done):
        if _STAGE < 2:
            return None
        ln2a = emit_layernorm([(dve, tq0done)], (0,), alt=True)
        emit_ag1_half(0, ln2a)
        return ln2a

    out_src = emit_ffn(True, ln1, mid_hook=_ln2a_hook)
    if _TAPS:
        B.dma("gpsimd", lambda e: e.dma_start(out=t_xa[:, :], in_=X[:, 0:1024]),
              waits=[(dve, _dtb0_snap[0])])
        B.dma("gpsimd", lambda e: e.dma_start(out=t_w1q[:, :], in_=w1s[0, :, :]),
              waits=[(dve, out_src)])
        B.dma("gpsimd", lambda e: e.dma_start(out=t_w3q[:, :], in_=w3s[0, :, :]),
              waits=[(dve, out_src)])
        B.dma("gpsimd", lambda e: e.dma_start(out=t_hid[:, :], in_=HID[:, 0, :]),
              waits=[(dve, out_src)])
    if _STAGE >= 2:
        ln2b = emit_layernorm([(dve, out_src)], (1,), alt=True)
        emit_ag1_half(1, ln2b)
        B.fence()
        out_src = emit_attention()
    if _STAGE >= 3:
        B.fence()
        ln3 = emit_layernorm([(dve, out_src)], alt=True)
        out_src = emit_ffn(False, ln3)

    final_dve = dve.n
    dy = B.new_ctr("dy")
    for dc in range(DC):
        B.dma_on("sync", dy, lambda e, dc=dc: e.dma_start(
            out=y[dc * 128:(dc + 1) * 128, :],
            in_=X[:, dc * TC:(dc + 1) * TC]), waits=[(dve, final_dve)])
    out_done = dy.n
    B.op("sync", lambda e: e.wait_ge(dy.h, out_done))
    if _TAPS:
        B.op("gpsimd", lambda e: e.wait_ge(dG.h, dG.n))

    with nc.Block() as block:
        block.sync(lambda e: B.replay("sync", e))
        block.scalar(lambda e: B.replay("scalar", e))
        block.gpsimd(lambda e: B.replay("gpsimd", e))
        block.tensor(lambda e: B.replay("tensor", e))
        block.vector(lambda e: B.replay("vector", e))

    stack.__exit__(None, None, None)
    return nc


_NEG = np.float32(-1e30)


def _masks():
    qi = np.arange(128)[:, None]
    kk = np.arange(512)[None, :]
    ma = np.concatenate(
        [np.where(kk <= o * 128 + qi, np.float32(0.0), _NEG)
         for o in range(4)], axis=1).astype(ml_dtypes.bfloat16)
    ki = np.arange(128)[:, None]
    qq = np.arange(512)[None, :]
    mb = np.concatenate(
        [np.where(qq >= v * 128 + ki, np.float32(0.0), _NEG)
         for v in range(4)], axis=1).astype(ml_dtypes.bfloat16)
    return ma, mb


def kernel(L0_attn_x, L0_attn_Wq, L0_attn_Wk, L0_attn_Wv, L0_attn_Wo,
           L0_attn_gq, L0_attn_gk, L0_ffn_w1, L0_ffn_w2, L0_ffn_w3):
    f = np.float32
    x = np.asarray(L0_attn_x, f)
    Wq = np.asarray(L0_attn_Wq, f); Wk = np.asarray(L0_attn_Wk, f)
    Wv = np.asarray(L0_attn_Wv, f); Wo = np.asarray(L0_attn_Wo, f)
    gq = np.asarray(L0_attn_gq, f); gk = np.asarray(L0_attn_gk, f)
    w1 = np.asarray(L0_ffn_w1, f); w2 = np.asarray(L0_ffn_w2, f)
    w3 = np.asarray(L0_ffn_w3, f)

    Ct = np.ascontiguousarray
    w1Tm = Ct(w1.T); w2Tm = Ct(w2.T); w3Tm = Ct(w3.T)
    WqT = Ct(Wq.T); WkT = Ct(Wk.T); WvT = Ct(Wv.T); WoT = Ct(Wo.T)
    ma, mb = _masks()
    shared = {
        "w1T": w1Tm, "w2T": w2Tm, "w3T": w3Tm, "woT": WoT,
        "gq": Ct(gq.reshape(HD, 1)), "gk": Ct(gk.reshape(HD, 1)),
        "maskB": mb,
        "ones_b": np.ones((128, 1), ml_dtypes.bfloat16),
        "ones_f": np.ones((128, 1), f),
        "onesr_f": np.ones((1, 128), f),
        "onesr_b": np.ones((1, 128), ml_dtypes.bfloat16),
        "ide": np.eye(128, dtype=f),
        "ide_b": np.eye(128, dtype=ml_dtypes.bfloat16),
    }
    in_maps = []
    for r in range(NCORES):
        b, c = r // 4, r % 4
        m = dict(shared)
        m["xT"] = Ct(x[b, c * TC:(c + 1) * TC, :].T)
        hcs = [h for h in range(r, HC, 8)]
        g1 = np.zeros((D, 768), f); g2 = np.zeros((D, 768), f)
        g3 = np.zeros((768, D), f)
        for i, h in enumerate(hcs):
            g1[:, i * 128:(i + 1) * 128] = w1Tm[:, h * 128:(h + 1) * 128]
            g2[:, i * 128:(i + 1) * 128] = w2Tm[:, h * 128:(h + 1) * 128]
            g3[i * 128:(i + 1) * 128, :] = w3Tm[h * 128:(h + 1) * 128, :]
        m["g1s"] = g1; m["g2s"] = g2; m["g3s"] = g3
        m["wqT"] = Ct(WqT[:, r * 256:(r + 1) * 256])
        m["wkT"] = Ct(WkT[:, r * 256:(r + 1) * 256])
        m["wvT"] = Ct(WvT[:, r * 256:(r + 1) * 256])
        in_maps.append(m)

    nc = build_program()
    res = run_bass_kernel_spmd(nc, in_maps, core_ids=list(range(NCORES)))
    out = np.empty((NB, SEQ, D), f)
    for r in range(NCORES):
        b, c = r // 4, r % 4
        out[b, c * TC:(c + 1) * TC, :] = np.asarray(res.results[r]["y"]).T
    if _TAPS:
        _sys.modules[__name__]._tap_results = res.results
    return out


# version 11
# speedup vs baseline: 1.0148x; 1.0122x over previous
import numpy as np
import ml_dtypes
import sys as _sys
import concourse.bass as bass
import concourse.mybir as mybir
from concourse.bass_utils import run_bass_kernel_spmd

F32 = mybir.dt.float32
BF16 = mybir.dt.bfloat16
AF = mybir.ActivationFunctionType
ALU = mybir.AluOpType
AX = mybir.AxisListType.X

NCORES = 8
D = 2048
HF = 5632
HC = HF // 128
DC = D // 128
TC = 1024
SEQ = 4096
NB = 2
HD = 128
NW = HF * D
EPS_LN = 1e-5
EPS_RMS = 1e-5
EPS_Q = 1e-8
ISQ = 0.08838834764831845

_STAGE = 3
_TAPS = False


class Ring:
    def __init__(self, n):
        self.size = n
        self.rel = [None] * n
        self.i = 0

    def next(self):
        s = self.i % self.size
        self.i += 1
        return s, list(self.rel[s] or [])

    def release(self, s, waits):
        self.rel[s] = list(waits)


class Ctr:
    __slots__ = ("h", "name", "n")


class DmaRing:

    def __init__(self, B, queue, name, depth):
        self.B = B
        self.queue = queue
        self.sems = [B.new_ctr(f"{name}{i}") for i in range(depth)]
        self.rel = [None] * depth
        self.i = 0
        self.depth = depth

    def load(self, fn, waits=()):
        s = self.i % self.depth
        self.i += 1
        w = list(self.rel[s] or []) + list(waits)
        v = self.B.op(self.queue, fn, waits=w, inc=self.sems[s], inck=16)
        return s, [(self.sems[s], v)]

    def load_slot(self, s, fn, waits=()):
        v = self.B.op(self.queue, fn, waits=list(waits),
                      inc=self.sems[s], inck=16)
        return [(self.sems[s], v)]

    def release(self, s, waits):
        self.rel[s] = list(waits)

    def totals(self):
        return [(c, c.n) for c in self.sems]


class Builder:
    ENGS = ("sync", "scalar", "gpsimd", "tensor", "vector")

    def __init__(self, nc, stack):
        self.nc = nc
        self.stack = stack
        self.streams = {e: [] for e in self.ENGS}
        self.ctrs = {}
        for name in ("dW", "dA", "dG", "pe", "dve", "act", "pool", "cc"):
            self.new_ctr(name)

    def new_ctr(self, name):
        c = Ctr.__new__(Ctr)
        c.h = self.stack.enter_context(self.nc.semaphore(name))
        c.name = name
        c.n = 0
        self.ctrs[name] = c
        return c

    def op(self, eng, fn, waits=(), inc=None, inck=1, hwaits=()):
        w = [(c, v) for (c, v) in waits if c is not None and v > 0]
        hw = [(c, v) for (c, v) in hwaits if c is not None and v > 0]
        self.streams[eng].append((w, hw, fn, inc, inck))
        if inc is not None:
            inc.n += inck
            return inc.n
        return None

    def dma(self, queue, fn, waits=()):
        qc = {"sync": "dW", "scalar": "dA", "gpsimd": "dG"}[queue]
        return self.op(queue, fn, waits, inc=self.ctrs[qc], inck=16)

    def dma_on(self, queue, ctr, fn, waits=()):
        return self.op(queue, fn, waits, inc=ctr, inck=16)

    def fence(self):
        snap = [(c, c.n) for c in self.ctrs.values()]
        for eng in self.ENGS:
            self.op(eng, lambda e: None, waits=snap, hwaits=snap)

    def replay(self, eng_name, e):
        own = {"sync": None, "scalar": "act", "gpsimd": "pool",
               "tensor": "pe", "vector": "dve"}[eng_name]
        waited = {}
        for w, hw, fn, inc, inck in self.streams[eng_name]:
            for c, v in w:
                if c.name == own:
                    continue
                if waited.get(c.name, 0) >= v:
                    continue
                e.wait_ge(c.h, v)
                waited[c.name] = v
            for c, v in hw:
                if waited.get(c.name, 0) >= v:
                    continue
                e.wait_ge(c.h, v)
                waited[c.name] = v
            inst = fn(e)
            if inc is not None:
                inst.then_inc(inc.h, inck)


def build_program():
    nc = bass.Bass("TRN2", target_bir_lowering=False, debug=False,
                   detect_race_conditions=False)

    def din(name, shape, dt=F32):
        return nc.dram_tensor(name, shape, dt, kind="ExternalInput")

    xT = din("xT", [D, TC])
    w1T = din("w1T", [D, HF]); w2T = din("w2T", [D, HF]); w3T = din("w3T", [HF, D])
    g1s = din("g1s", [D, 768]); g2s = din("g2s", [D, 768])
    g3s = din("g3s", [768, D])
    wqT = din("wqT", [D, 256]); wkT = din("wkT", [D, 256]); wvT = din("wvT", [D, 256])
    woT = din("woT", [D, D])
    gq_in = din("gq", [HD, 1]); gk_in = din("gk", [HD, 1])
    maskB_in = din("maskB", [128, 2048], BF16)
    ones_b_in = din("ones_b", [128, 1], BF16)
    ones_f_in = din("ones_f", [128, 1])
    onesr_f_in = din("onesr_f", [1, 128])
    onesr_b_in = din("onesr_b", [1, 128], BF16)
    ide_in = din("ide", [128, 128])
    ideb_in = din("ide_b", [128, 128], BF16)

    y = nc.dram_tensor("y", [D, TC], F32, kind="ExternalOutput")
    if _TAPS:
        t_thp = nc.dram_tensor("t_thp", [128, 8], F32, kind="ExternalOutput")
        t_xln = nc.dram_tensor("t_xln", [128, DC * TC], BF16,
                               kind="ExternalOutput")
        t_w1q = nc.dram_tensor("t_w1q", [128, D], BF16, kind="ExternalOutput")
        t_w3q = nc.dram_tensor("t_w3q", [128, D], BF16, kind="ExternalOutput")
        t_hid = nc.dram_tensor("t_hid", [128, 512], BF16, kind="ExternalOutput")
        t_xa = nc.dram_tensor("t_xa", [128, 1024], F32, kind="ExternalOutput")

    agq_in = nc.dram_tensor("agq_in", [256, 3 * D], BF16, kind="Internal")
    agq_in_b = nc.dram_tensor("agq_in_b", [512, 3 * D], BF16, kind="Internal")
    agq_out_a = nc.dram_tensor("agq_out_a", [NCORES * 256, 3 * D], BF16,
                               kind="Internal", addr_space="Shared")
    agq_out_b = nc.dram_tensor("agq_out_b", [NCORES * 512, 3 * D], BF16,
                               kind="Internal", addr_space="Shared")
    agx_in_a = nc.dram_tensor("agx_in_a", [D, 512], BF16, kind="Internal")
    agx_in_b = nc.dram_tensor("agx_in_b", [D, 512], BF16, kind="Internal")
    agx_out_a = nc.dram_tensor("agx_out_a", [NCORES * D, 512], BF16,
                               kind="Internal", addr_space="Shared")
    agx_out_b = nc.dram_tensor("agx_out_b", [NCORES * D, 512], BF16,
                               kind="Internal", addr_space="Shared")
    ago_in = nc.dram_tensor("ago_in", [256, NB * SEQ], BF16, kind="Internal")
    ago_out = nc.dram_tensor("ago_out", [D, NB * SEQ], BF16,
                             kind="Internal", addr_space="Shared")
    woS = nc.dram_tensor("woS", [D, D], BF16, kind="Internal")
    gar_in = nc.dram_tensor("gar_in", [1, 8], F32, kind="Internal")
    gar_out = nc.dram_tensor("gar_out", [1, 8], F32, kind="Internal")

    from contextlib import ExitStack
    stack = ExitStack()
    stack.__enter__()
    B = Builder(nc, stack)
    C = B.ctrs
    dW, dA, dG = C["dW"], C["dA"], C["dG"]
    pe, dve, act, pool, cc = C["pe"], C["dve"], C["act"], C["pool"], C["cc"]

    al = nc.alloc_sbuf_tensor
    X = al("X", [128, DC * TC], F32)
    XLN = al("XLN", [128, DC * TC], BF16)
    MB = al("MB", [128, 2048], BF16)
    IDE = al("IDE", [128, 128], F32)
    IDEB = al("IDEB", [128, 128], BF16)
    ONE_B = al("ONE_B", [128, 1], BF16)
    ONE_F = al("ONE_F", [128, 1], F32)
    ONER_F = al("ONER_F", [1, 128], F32)
    ONER_B = al("ONER_B", [1, 128], BF16)
    GQS = al("GQS", [128, 1], F32); GKS = al("GKS", [128, 1], F32)
    THP = al("THP", [128, 8], F32)
    GSUM = al("GSUM", [128, 4], F32)
    GROW = al("GROW", [1, 16], F32)
    COLS = al("COLS", [128, 24], F32)
    DIAG = al("DIAG", [128, 128], F32)
    TMPC = al("TMPC", [128, 1], F32)

    ARENA_BYTES = 88 * 1024
    arena = al("ARENA", [128, ARENA_BYTES], mybir.dt.uint8)
    abase = nc.lookup_mloc(arena).addr
    _aid = [0]

    def at(shape, dt, off):
        _aid[0] += 1
        return nc.alloc_sbuf_tensor_at(f"ar{_aid[0]}", shape, dt,
                                       offset=abase + off)

    STG = at([128, 2, 2048], F32, 0)
    U2S = at([128, 2048], BF16, 16384)
    W1Q = at([128, 2, 2048], BF16, 20480)
    W2Q = at([128, 2, 2048], BF16, 28672)
    W3Q = at([128, 2, 1024], BF16, 36864)
    HID = at([128, HC, 512], BF16, 40960)
    SG = at([128, 2, 512], BF16, 86016)

    TMPF = al("TMPF", [128, 512], F32)
    X2S = al("X2S", [128, 1024], BF16)
    MBR = al("MBR", [128, 1024], BF16)
    RBR = al("RBR", [128, 1024], BF16)


    GST = STG

    QA = at([128, SEQ], BF16, 0)
    QB = at([128, SEQ], BF16, 8192)
    KA = at([128, SEQ], BF16, 16384)
    KB = at([128, SEQ], BF16, 24576)
    VV = at([128, 2 * SEQ], BF16, 32768)
    XGR = at([128, 6, 512], BF16, 49152)
    PBR = at([128, 2, 1024], BF16, 55296)
    PSB = at([128, 2, 512], BF16, 59392)
    Q2S = at([128, 2, 512], BF16, 61440)
    OSTB = at([128, 2, 512], BF16, 63488)
    LBS = at([128, 512], F32, 65536)
    LROW = at([1, 512], F32, 67584)
    LRB = at([1, 512], BF16, 69632)
    R0 = at([1, 1024], F32, 70656)
    R1 = at([1, 1024], F32, 74752)
    RSR = at([1, 512], F32, 78848)
    RSB = at([1, 512], BF16, 80896)
    XBH = at([128, 2, 512], BF16, 61440)
    R2 = at([1, 1024], F32, 81920)

    xlnb = nc.lookup_mloc(XLN).addr
    WQ = nc.alloc_sbuf_tensor_at("WQ", [128, DC, 256], BF16, offset=xlnb)
    WK = nc.alloc_sbuf_tensor_at("WK", [128, DC, 256], BF16, offset=xlnb + 8192)
    WV = nc.alloc_sbuf_tensor_at("WV", [128, DC, 256], BF16,
                                 offset=xlnb + 16384)

    OM = at([128, DC, TC], BF16, 0)
    WOS = at([128, 4, 2048], BF16, 32768)

    PS1 = nc.alloc_psum_tensor("PS1", [128, 2048], F32)
    PS2 = nc.alloc_psum_tensor("PS2", [128, 2048], F32)

    def b1(i):
        return PS1[:, 512 * i:512 * (i + 1)]

    def b2(i):
        return PS2[:, 512 * i:512 * (i + 1)]

    def MM(out_ap, lhsT_ap, rhs_ap, start, stop, waits=(), inc=None):
        return B.op("tensor",
                    lambda e, o=out_ap, l=lhsT_ap, r=rhs_ap, s=start, t=stop:
                    e.matmul(o, l, r, start=s, stop=t), waits, inc=inc)

    def xsl(dc, tq):
        return X[:, dc * TC + tq * 512: dc * TC + (tq + 1) * 512]

    def xlsl(dc, tq):
        return XLN[:, dc * TC + tq * 512: dc * TC + (tq + 1) * 512]

    dx = B.new_ctr("dx")
    for dc in range(DC):
        B.dma_on("sync", dx, lambda e, dc=dc: e.dma_start(
            out=X[:, dc * TC:(dc + 1) * TC], in_=xT[dc * 128:(dc + 1) * 128, :]))
    x_loaded = dx.n
    xw = (dx, x_loaded)

    dcst = B.new_ctr("dcst")
    for dst, src in ((MB, maskB_in), (IDE, ide_in),
                     (IDEB, ideb_in), (ONER_B, onesr_b_in),
                     (ONE_B, ones_b_in), (ONE_F, ones_f_in),
                     (ONER_F, onesr_f_in), (GQS, gq_in), (GKS, gk_in)):
        B.dma_on("scalar", dcst, lambda e, d=dst, s=src:
                 e.dma_start(out=d[:], in_=s[:]))
    const_loaded = dcst.n
    cw = (dcst, const_loaded)
    B.op("vector", lambda e: e.tensor_scalar_mul(GQS[:], GQS[:], ISQ),
         waits=[cw], inc=dve)
    B.op("vector", lambda e: e.memset(GROW[:], 0.0), inc=dve)
    B.op("vector", lambda e: e.memset(THP[:], 0.0), inc=dve)
    B.op("vector", lambda e: e.memset(THP[:, 3:4], -0.5), inc=dve)
    B.op("vector", lambda e: e.memset(THP[:, 7:8], -20.0), inc=dve)

    def emit_gamma():
        ring = DmaRing(B, "sync", "gst", 2)
        dgar = B.new_ctr("dgar")
        plan = []
        for j, (g, rows, cols) in enumerate(
                [(g1s, D, 768), (g2s, D, 768), (g3s, 768, D)]):
            per = max(1, 2048 // cols)
            r0 = 0
            while r0 < rows // 128:
                n = min(per, rows // 128 - r0)
                plan.append((g, r0, n, cols, j))
                r0 += n
        jcols = {0: [], 1: [], 2: []}
        for ci, (g, r0, n, cols, j) in enumerate(plan):
            def ldfn(e, g=g, r0=r0, n=n, cols=cols, s=None):
                return e.dma_start(
                    out=GST[:, s, 0:n * cols].rearrange("p (a h) -> p a h", a=n),
                    in_=g[r0 * 128:(r0 + n) * 128, :].rearrange(
                        "(a p) h -> p a h", p=128))
            slot = ring.i % ring.depth
            _, ldw = ring.load(
                lambda e, f=ldfn, s=slot: f(e, s=s))
            rd = B.op("vector", lambda e, s=slot, n=n, cols=cols, ci=ci:
                      e.reduce_sum(COLS[:, ci:ci + 1], GST[:, s, 0:n * cols],
                                   AX, apply_absolute_value=True),
                      waits=ldw, inc=dve)
            jcols[j].append(ci)
            ring.release(slot, [(dve, rd)])
        allred = dve.n
        for j in range(3):
            c0, c1 = jcols[j][0], jcols[j][-1] + 1
            gs = B.op("vector", lambda e, j=j, c0=c0, c1=c1: e.reduce_sum(
                GSUM[:, j:j + 1], COLS[:, c0:c1], AX),
                hwaits=[(dve, allred)], inc=dve)
            dsn = B.op("vector", lambda e, j=j: e.tensor_scalar_mul(
                DIAG[:], IDE[:], GSUM[:, j:j + 1]),
                waits=[cw, (pe, pe.n)],
                hwaits=[(dve, gs)], inc=dve)
            msn = MM(PS1[0:1, 128 * j:128 * j + 128], ONE_F[:], DIAG[:],
                     True, True, waits=[(dve, dsn)], inc=pe)
            B.op("vector", lambda e, j=j: e.reduce_sum(
                GROW[0:1, j:j + 1], PS1[0:1, 128 * j:128 * j + 128], AX),
                waits=[(pe, msn)], inc=dve)
        gsn = dve.n
        s1 = B.dma_on("gpsimd", dgar, lambda e: e.dma_start(
            out=gar_in[:, :], in_=GROW[0:1, 0:8]), waits=[(dve, gsn)])
        c1 = B.op("gpsimd", lambda e: e.collective_compute(
            "AllReduce", ALU.add, replica_groups=[list(range(NCORES))],
            ins=[gar_in[:, :].opt()], outs=[gar_out[:, :].opt()]),
            waits=[(dgar, s1)], inc=cc)
        s2 = B.dma_on("gpsimd", dgar, lambda e: e.dma_start(
            out=GROW[0:1, 8:16], in_=gar_out[:, :]), waits=[(cc, c1)])
        tsn = B.op("vector", lambda e: e.tensor_scalar(
            GROW[0:1, 8:16], GROW[0:1, 8:16], 0.5 / NW, 0.5 * EPS_Q,
            ALU.mult, ALU.add), waits=[(dgar, s2)], inc=dve)
        for j in range(3):
            msn = MM(PS1[:, 512 + j:513 + j], ONER_F[:], GROW[0:1, 8 + j:9 + j],
                     True, True, waits=[(dve, tsn)], inc=pe)
            B.op("vector", lambda e, j=j: e.tensor_copy(
                THP[:, j:j + 1], PS1[:, 512 + j:513 + j]),
                waits=[(pe, msn)], inc=dve)
            B.op("vector", lambda e, j=j: e.tensor_scalar_mul(
                THP[:, 4 + j:5 + j], PS1[:, 512 + j:513 + j], -1.0), inc=dve)
        return dve.n

    gamma_done = emit_gamma()

    def emit_layernorm(prereq, tqs=(0, 1), alt=False):
        xbh = XBH2 if alt else XBH
        prev_mm = {}
        cnt = 0
        for dc in range(DC):
            for tq in tqs:
                i = cnt
                cnt += 1
                cs = B.op("scalar", lambda e, dc=dc, tq=tq, i=i: e.activation(
                    xbh[:, i % 2, :], xsl(dc, tq), AF.Identity),
                    waits=list(prereq) + [(pe, prev_mm.get(i - 2, 0))],
                    inc=act)
                sq = B.op("scalar", lambda e, dc=dc, tq=tq, i=i: e.activation(
                    X2S[:, (i % 2) * 512:(i % 2 + 1) * 512], xsl(dc, tq),
                    AF.Square), inc=act)
                MM(PS2[0:1, 1024 + tq * 512:1536 + tq * 512], ONE_B[:],
                   xbh[:, i % 2, :], start=(dc == 0), stop=(dc == DC - 1),
                   waits=[(act, cs)])
                m = MM(PS1[0:1, 1024 + tq * 512:1536 + tq * 512], ONE_B[:],
                       X2S[:, (i % 2) * 512:(i % 2 + 1) * 512],
                       start=(dc == 0), stop=(dc == DC - 1),
                       waits=[(act, sq)], inc=pe)
                prev_mm[i] = m
        stat_done = pe.n
        for tq in tqs:
            if alt:
                r0, r1, r2 = R0b[0:1, :], R1b[0:1, :], R2b[0:1, :]
            else:
                r0 = R0[0:1, tq * 512:(tq + 1) * 512]
                r1 = R1[0:1, tq * 512:(tq + 1) * 512]
                r2 = R2[0:1, tq * 512:(tq + 1) * 512]
            s1 = B.op("vector", lambda e, r0=r0, tq=tq: e.tensor_scalar_mul(
                r0, PS2[0:1, 1024 + tq * 512:1536 + tq * 512], 1.0 / D),
                waits=[(pe, stat_done)], inc=dve)
            s2 = B.op("vector", lambda e, r1=r1, tq=tq: e.tensor_scalar_mul(
                r1, PS1[0:1, 1024 + tq * 512:1536 + tq * 512], 1.0 / D),
                inc=dve)
            s3 = B.op("vector", lambda e, r0=r0, r2=r2: e.tensor_mul(
                r2, r0, r0), hwaits=[(dve, s1)], inc=dve)
            s4 = B.op("vector", lambda e, r1=r1, r2=r2: e.tensor_sub(
                r1, r1, r2), hwaits=[(dve, s3)], inc=dve)
            vsn = B.op("vector", lambda e, r1=r1: e.tensor_scalar_add(
                r1, r1, EPS_LN), hwaits=[(dve, s4)], inc=dve)
            ssn = B.op("scalar", lambda e, r1=r1: e.activation(
                r1, r1, AF.Ln), waits=[(dve, vsn)], inc=act)
            rsn2 = B.op("scalar", lambda e, r1=r1: e.activation(
                r1, r1, AF.Exp, scale=-0.5), inc=act)
            rsn = B.op("vector", lambda e: e.engine_nop(),
                       waits=[(act, rsn2)], inc=dve)
            m1 = MM(b2(tq), ONER_F[:], r0, True, True,
                    waits=[(dve, rsn)], inc=pe)
            B.op("vector", lambda e, tq=tq: e.tensor_copy(
                MBR[:, tq * 512:(tq + 1) * 512], b2(tq)),
                waits=[(pe, m1)], inc=dve)
            m2 = MM(b2(2 + tq), ONER_F[:], r1, True, True, inc=pe)
            B.op("vector", lambda e, tq=tq: e.tensor_copy(
                RBR[:, tq * 512:(tq + 1) * 512], b2(2 + tq)),
                waits=[(pe, m2)], inc=dve)
            bc = dve.n
            for dc in range(DC):
                B.op("vector", lambda e, dc=dc, tq=tq: e.tensor_sub(
                    TMPF[:, :], xsl(dc, tq),
                    MBR[:, tq * 512:(tq + 1) * 512]),
                    waits=[(dve, bc)], inc=dve)
                B.op("vector", lambda e, dc=dc, tq=tq: e.tensor_mul(
                    xlsl(dc, tq), TMPF[:, :],
                    RBR[:, tq * 512:(tq + 1) * 512]), inc=dve)
        return dve.n

    stg = Ring(2)
    stgd = DmaRing(B, "sync", "stg", 2)
    qsc = DmaRing(B, "scalar", "qsc", 2)
    agq_ready = [0, 0]
    qsc_early = [[]]

    QOBT = al("QOBT", [128, 2, 2048], BF16)
    qobase = nc.lookup_mloc(QOBT).addr
    R0b = nc.alloc_sbuf_tensor_at("R0b", [1, 512], F32, offset=qobase)
    R1b = nc.alloc_sbuf_tensor_at("R1b", [1, 512], F32, offset=qobase + 2048)
    R2b = nc.alloc_sbuf_tensor_at("R2b", [1, 512], F32, offset=qobase + 4096)
    XBH2 = nc.alloc_sbuf_tensor_at("XBH2", [128, 2, 512], BF16,
                                   offset=qobase + 6144)



    def emit_qshard():
        QOB = [QOBT[:, 0, :], QOBT[:, 1, :]]
        SGN = U2S[:, :]
        prev_q = [[], []]
        for u in range(6):
            for j, g in enumerate((g1s, g2s, g3s)):
                ss = stgd.i % stgd.depth
                if j == 2:
                    _, ldw = stgd.load(lambda e, u=u, s=ss: e.dma_start(
                        out=STG[:, s, :],
                        in_=g3s[u * 128:(u + 1) * 128, :]),
                        waits=[(dve, gamma_done)])
                else:
                    _, ldw = stgd.load(lambda e, g=g, u=u, s=ss: e.dma_start(
                        out=STG[:, s, :].rearrange("p (a h) -> p a h", a=16),
                        in_=g[:, u * 128:(u + 1) * 128].rearrange(
                            "(a p) h -> p a h", p=128)),
                        waits=[(dve, gamma_done)])
                qi = (u * 3 + j) % 2
                u1 = B.op("vector", lambda e, s=ss, j=j, qi=qi:
                          e.tensor_scalar(
                    QOB[qi], STG[:, s, :], THP[:, j:j + 1], THP[:, 3:4],
                    ALU.is_ge, ALU.add),
                    waits=ldw + [(dve, gamma_done)] + prev_q[qi], inc=dve)
                sg = B.op("scalar", lambda e, s=ss, j=j: e.activation(
                    SGN, STG[:, s, :], AF.Sign, bias=THP[:, j:j + 1]),
                    waits=ldw + [(dve, _qd_used[0])], inc=act)
                qd = B.op("vector", lambda e, qi=qi: e.scalar_tensor_tensor(
                    QOB[qi], SGN, 0.5, QOB[qi], ALU.mult, ALU.add),
                    waits=[(act, sg)], inc=dve)
                _qd_used[0] = qd
                st = qsc.load_slot(qi, lambda e, qi=qi, u=u, j=j: e.dma_start(
                    out=(agq_in[u * 128:(u + 1) * 128, j * D:(j + 1) * D]
                         if u < 2 else
                         agq_in_b[(u - 2) * 128:(u - 1) * 128,
                                  j * D:(j + 1) * D]),
                    in_=QOB[qi]), waits=[(dve, qd)])
                prev_q[qi] = st
                stgd.release(ss, [(dve, u1), (act, sg)])
            if u == 1:
                qsc_early[0] = qsc.totals()
        aga = B.op("gpsimd", lambda e: e.collective_compute(
            "AllGather", ALU.bypass, replica_groups=[list(range(NCORES))],
            ins=[agq_in[:, :].opt()], outs=[agq_out_a[:, :].opt()]),
            waits=qsc_early[0], inc=cc)
        agb = B.op("gpsimd", lambda e: e.collective_compute(
            "AllGather", ALU.bypass, replica_groups=[list(range(NCORES))],
            ins=[agq_in_b[:, :].opt()], outs=[agq_out_b[:, :].opt()]),
            waits=qsc.totals(), inc=cc)
        agq_ready[0] = aga
        agq_ready[1] = agb

    _qd_used = [0]
    _sgn_used = [0]

    r1 = Ring(2)
    r2 = Ring(2)
    w1l = DmaRing(B, "sync", "w1l", 2)
    w2l = DmaRing(B, "sync", "w2l", 2)
    w3l = DmaRing(B, "sync", "w3l", 2)
    r3 = Ring(2)
    p2evac = [0]

    def qsrc(hc):
        if hc < 16:
            return agq_out_a, (hc % 8) * 256 + (hc // 8) * 128
        return agq_out_b, (hc % 8) * 512 + (hc // 8 - 2) * 128

    def emit_ffn(first, ln_done, mid_hook=None):
        sig_done = {}
        gate_done = {}
        ln_gate = [ln_done]
        evacs_prev = [p2evac[0]]
        for tq in range(2):
            sig_done.clear()
            gate_done.clear()
            for hc in range(HC):
                consume = {}
                for widx, wqbuf, wring, ldr in (
                        (0, W1Q, r1, w1l), (1, W2Q, r2, w2l)):
                    qs, qw = wring.next()
                    ldw = ldr.load_slot(qs, lambda e, hc=hc, q=qs, wq=wqbuf,
                                        widx=widx: e.dma_start(
                        out=wq[:, q, :],
                        in_=qsrc(hc)[0][qsrc(hc)[1]:qsrc(hc)[1] + 128,
                                        widx * D:(widx + 1) * D]),
                        waits=list(qw) + [(cc, agq_ready[0 if hc < 16 else 1])])
                    consume[widx] = (qs, ldw)
                xb = hc % 2
                q1, w1w = consume[0]
                q2, w2w = consume[1]
                for dc in range(DC):
                    MM(b1(xb), W1Q[:, q1, dc * 128:(dc + 1) * 128],
                       xlsl(dc, tq), start=(dc == 0), stop=(dc == DC - 1),
                       waits=(w1w + [(dve, ln_gate[0]),
                                     (act, sig_done.get(hc - 2, 0)),
                                     (dve, evacs_prev[0])])
                       if dc == 0 else (),
                       inc=(pe if dc == DC - 1 else None))
                x1g = pe.n
                for dc in range(DC):
                    MM(b1(2 + xb), W2Q[:, q2, dc * 128:(dc + 1) * 128],
                       xlsl(dc, tq), start=(dc == 0), stop=(dc == DC - 1),
                       waits=(w2w + [(dve, gate_done.get(hc - 2, 0)),
                                     (dve, evacs_prev[0])])
                       if dc == 0 else (),
                       inc=(pe if dc == DC - 1 else None))
                x2g = pe.n
                r1.release(q1, [(pe, x1g)])
                r2.release(q2, [(pe, x2g)])
                sg = B.op("scalar", lambda e, xb=xb: e.activation(
                    SG[:, xb, :], b1(xb), AF.Sigmoid),
                    waits=[(pe, x1g), (dve, gate_done.get(hc - 2, 0))],
                    inc=act)
                hd_ = B.op("vector", lambda e, xb=xb, hc=hc: e.tensor_tensor(
                    HID[:, hc, :], b1(2 + xb), SG[:, xb, :], ALU.mult),
                    waits=[(pe, x2g), (act, sg)], inc=dve)
                sig_done[hc] = sg
                gate_done[hc] = hd_
            p1_dve = dve.n
            for dtb in range(2):
                grp = 0
                for hc in range(HC):
                    s3, w3w_ = r3.next()
                    ldw = w3l.load_slot(s3, lambda e, hc=hc, dtb=dtb, s=s3:
                                        e.dma_start(
                        out=W3Q[:, s, :],
                        in_=qsrc(hc)[0][qsrc(hc)[1]:qsrc(hc)[1] + 128,
                                        2 * D + dtb * 1024:
                                        2 * D + (dtb + 1) * 1024]),
                        waits=list(w3w_) + [(cc, agq_ready[0 if hc < 16 else 1])])
                    for dtl in range(8):
                        ps = b1(dtl) if dtl < 4 else b2(dtl - 4)
                        grp = MM(ps, W3Q[:, s3, dtl * 128:(dtl + 1) * 128],
                                 HID[:, hc, :],
                                 start=(hc == 0), stop=(hc == HC - 1),
                                 waits=(ldw + [(dve, p1_dve),
                                               (dve, evacs_prev[0])])
                                 if dtl == 0 else ldw,
                                 inc=(pe if dtl == 7 else None))
                    r3.release(s3, [(pe, grp)])
                for dtl in range(8):
                    dt = dtb * 8 + dtl
                    ps = b1(dtl) if dtl < 4 else b2(dtl - 4)
                    B.op("vector", lambda e, dt=dt, ps=ps, tq=tq:
                         e.scalar_tensor_tensor(
                        xsl(dt, tq), ps, 0.5, xsl(dt, tq), ALU.mult, ALU.add),
                        waits=[(pe, grp)], inc=dve)
                evacs_prev[0] = dve.n
            if tq == 0 and mid_hook is not None:
                r = mid_hook(dve.n)
                if r is not None:
                    ln_gate[0] = r
        p2evac[0] = dve.n
        return dve.n

    dagx = B.new_ctr("dagx")
    ag1cc = [0, 0]

    def emit_ag1_half(tq, ln_done):
        gin = (agx_in_a, agx_in_b)[tq]
        gout = (agx_out_a, agx_out_b)[tq]
        for dc in range(DC):
            B.dma_on("scalar", dagx, lambda e, dc=dc, tq=tq, gin=gin:
                     e.dma_start(
                out=gin[dc * 128:(dc + 1) * 128, :],
                in_=xlsl(dc, tq)), waits=[(dve, ln_done)])
        ag = B.op("gpsimd", lambda e, gin=gin, gout=gout:
                  e.collective_compute(
            "AllGather", ALU.bypass, replica_groups=[list(range(NCORES))],
            ins=[gin[:, :].opt()], outs=[gout[:, :].opt()]),
            waits=[(dagx, dagx.n)], inc=cc)
        ag1cc[tq] = ag

    def emit_attention():
        contrib = dagx.n
        dwqkv = B.new_ctr("dwqkv")
        for w_in, wbuf in ((wqT, WQ), (wkT, WK), (wvT, WV)):
            B.dma_on("gpsimd", dwqkv, lambda e, w_in=w_in, wbuf=wbuf:
                     e.dma_start(
                out=wbuf[:], in_=w_in.rearrange("(a p) c -> p a c", p=128)),
                waits=[(dagx, contrib)])
        wload = (dwqkv, dwqkv.n)
        dwos = B.new_ctr("dwos")
        B.dma_on("gpsimd", dwos, lambda e: e.dma_start(
            out=woS[:, :], in_=woT[:, :]), waits=[(dagx, contrib)])
        wosload = (dwos, dwos.n)

        xg = DmaRing(B, "scalar", "xg", 6)
        ostw = DmaRing(B, "gpsimd", "ostw", 2)
        scr_ring = Ring(3)
        pb_ring = Ring(2)
        ost_ring = Ring(2)
        passA_credit = [None, None]
        passB_credit = [None, None, None]
        qk_ready = [0]
        v_ready = [0]
        prev_rc = [0]
        prev_bc = [0]
        nmr_used = [0]
        lbs_used = [0]
        lrow_done = [0]
        ov_done = [0]
        pb_prev = [None]
        pair_cnt = [0]
        pair_credit = [None, None]
        psb_i = [0]
        psb_rel = [None, None]

        def emit_proj(b):
            if b:
                B.fence()
            prev_evac = {}
            vprev = 0
            for tq8 in (0, 2, 4, 6, 1, 3, 5, 7):
                vlast = 0
                for dc in range(DC):
                    s = xg.i % xg.depth
                    gout = (agx_out_a, agx_out_b)[tq8 % 2]
                    _, ldw = xg.load(lambda e, b=b, tq8=tq8, dc=dc, s=s,
                                     gout=gout: e.dma_start(
                        out=XGR[:, s, :],
                        in_=gout[(4 * b + tq8 // 2) * D + dc * 128:
                                 (4 * b + tq8 // 2) * D + (dc + 1) * 128, :]),
                        waits=[(cc, ag1cc[tq8 % 2])])
                    w0 = ldw + [wload]
                    if dc == 0:
                        for t in range(4):
                            w0 += prev_evac.get(t, [])
                        if vprev:
                            w0 += [(act, vprev)]
                    MM(b1(0), WQ[:, dc, 0:128], XGR[:, s, :],
                       start=(dc == 0), stop=(dc == DC - 1), waits=w0)
                    MM(b1(1), WQ[:, dc, 128:256], XGR[:, s, :],
                       start=(dc == 0), stop=(dc == DC - 1))
                    MM(b1(2), WK[:, dc, 0:128], XGR[:, s, :],
                       start=(dc == 0), stop=(dc == DC - 1))
                    MM(b1(3), WK[:, dc, 128:256], XGR[:, s, :],
                       start=(dc == 0), stop=(dc == DC - 1))
                    for t4 in range(4):
                        vlast = MM(PS2[:, t4 * 256:(t4 + 1) * 256],
                                   XGR[:, s, t4 * 128:(t4 + 1) * 128],
                                   WV[:, dc, :],
                                   start=(dc == 0), stop=(dc == DC - 1),
                                   inc=(pe if t4 == 3 else None))
                    xg.release(s, [(pe, vlast)])
                grp = vlast
                prev_rw = 0
                prev_ap = 0
                prev_rsmm = {}
                for t, (psb, dbuf, gcol) in enumerate(
                        ((b1(0), QA, GQS), (b1(1), QB, GQS),
                         (b1(2), KA, GKS), (b1(3), KB, GKS))):
                    sq = B.op("scalar", lambda e, psb=psb, t=t: e.activation(
                        Q2S[:, t % 2, :], psb, AF.Square),
                        waits=[(pe, grp), (pe, prev_rsmm.get(t - 2, 0))],
                        inc=act)
                    cp = B.op("scalar", lambda e, psb=psb, dbuf=dbuf, tq8=tq8:
                              e.activation(
                        dbuf[:, tq8 * 512:(tq8 + 1) * 512], psb, AF.Identity),
                        inc=act)
                    mm = MM(PS2[0:1, 1024:1536], ONE_B[:], Q2S[:, t % 2, :],
                            True, True,
                            waits=[(act, sq), (dve, prev_rw)], inc=pe)
                    prev_rsmm[t] = mm
                    rw = B.op("vector", lambda e: e.tensor_scalar(
                        RSR[0:1, :], PS2[0:1, 1024:1536],
                        1.0 / HD, EPS_RMS, ALU.mult, ALU.add),
                        waits=[(pe, mm), (act, prev_rc[0])], inc=dve)
                    sr = B.op("scalar", lambda e: e.activation(
                        RSR[0:1, :], RSR[0:1, :], AF.Ln),
                        waits=[(dve, rw)], inc=act)
                    rc = B.op("scalar", lambda e: e.activation(
                        RSB[0:1, :], RSR[0:1, :], AF.Exp, scale=-0.5),
                        waits=[(pe, prev_bc[0])], inc=act)
                    prev_rc[0] = rc
                    bc = MM(PS2[:, 1536:2048], ONER_B[:],
                            RSB[0:1, :], True, True,
                            waits=[(act, rc), (dve, prev_ap)], inc=pe)
                    prev_bc[0] = bc
                    ap = B.op("vector", lambda e, dbuf=dbuf, tq8=tq8, gcol=gcol:
                              e.scalar_tensor_tensor(
                        dbuf[:, tq8 * 512:(tq8 + 1) * 512],
                        dbuf[:, tq8 * 512:(tq8 + 1) * 512], gcol[:],
                        PS2[:, 1536:2048], ALU.mult, ALU.mult),
                        waits=[(pe, bc), (act, cp)], inc=dve)
                    prev_rw = rw
                    prev_ap = ap
                    prev_evac[t] = [(dve, ap), (act, cp)]
                for t4 in range(4):
                    vprev = B.op("scalar", lambda e, tq8=tq8, t4=t4:
                                 e.activation(
                        VV[:, (tq8 * 4 + t4) * 256:(tq8 * 4 + t4 + 1) * 256],
                        PS2[:, t4 * 256:(t4 + 1) * 256], AF.Identity),
                        waits=[(pe, grp)], inc=act)
            qk_ready[0] = dve.n
            v_ready[0] = act.n

        def emit_passB_qg(b, h, qg):
            Q = (QA, QB)[h]
            K = (KA, KB)[h]
            nkc = 4 * qg + 4
            pend = None
            grp = 0
            for pr in range(nkc // 2):
                half = pair_cnt[0] % 2
                pair_cnt[0] += 1
                hw0 = [(dve, qk_ready[0])] \
                    + ([pair_credit[half]] if pair_credit[half] else [])
                for sub in range(2):
                    kc = 2 * pr + sub
                    bank = b2(half * 2 + sub)
                    w = hw0 if sub == 0 else ()
                    if kc >= 4 * qg:
                        v = kc - 4 * qg
                        MM(bank, IDEB[:], MB[:, v * 512:(v + 1) * 512],
                           True, False, waits=w)
                        mm = MM(bank, K[:, kc * 128:(kc + 1) * 128],
                                Q[:, qg * 512:(qg + 1) * 512], False, True,
                                inc=(pe if sub == 1 else None))
                    else:
                        mm = MM(bank, K[:, kc * 128:(kc + 1) * 128],
                                Q[:, qg * 512:(qg + 1) * 512], True, True,
                                waits=w, inc=(pe if sub == 1 else None))
                if pb_prev[0] is not None:
                    pb_ring.release(pb_prev[0], [(pe, mm)])
                ps_, pw = pb_ring.next()
                ex = B.op("scalar", lambda e, half=half, p=ps_: e.activation(
                    PBR[:, p, :], PS2[:, half * 1024:(half + 1) * 1024],
                    AF.Exp, bias=THP[:, 7:8]),
                    waits=[(pe, mm)] + pw, inc=act)
                pair_credit[half] = (act, ex)
                sb = psb_i[0] % 2
                psb_i[0] += 1
                sm = B.op("vector", lambda e, p=ps_, sb=sb: e.tensor_add(
                    PSB[:, sb, :], PBR[:, p, 0:512], PBR[:, p, 512:1024]),
                    waits=[(act, ex)] + (psb_rel[sb] or []), inc=dve)
                if pend is not None:
                    pps, ppr, psb_, psm = pend
                    MM(PS1[0:1, 1536:2048], ONE_B[:], PSB[:, psb_, :],
                       start=(ppr == 0), stop=False,
                       waits=[(dve, psm)]
                       + ([(dve, lbs_used[0])] if ppr == 0 else []))
                    psb_rel[psb_] = [(pe, pe.n)]
                    for s2 in range(2):
                        pkc = 2 * ppr + s2
                        MM(PS1[:, 1024:1536],
                           VV[:, pkc * 256 + h * 128: pkc * 256 + h * 128 + 128],
                           PBR[:, pps, s2 * 512:(s2 + 1) * 512],
                           start=(pkc == 0), stop=False,
                           waits=([(dve, ov_done[0]), (act, v_ready[0])]
                                  if pkc == 0 else ()))
                pend = (ps_, pr, sb, sm)
            pps, ppr, psb_, psm = pend
            MM(PS1[0:1, 1536:2048], ONE_B[:], PSB[:, psb_, :],
               start=(ppr == 0), stop=True,
               waits=[(dve, psm)] + ([(dve, lbs_used[0])] if ppr == 0 else []))
            psb_rel[psb_] = [(pe, pe.n)]
            for s2 in range(2):
                pkc = 2 * ppr + s2
                grp = MM(PS1[:, 1024:1536],
                         VV[:, pkc * 256 + h * 128: pkc * 256 + h * 128 + 128],
                         PBR[:, pps, s2 * 512:(s2 + 1) * 512],
                         start=(pkc == 0), stop=(s2 == 1),
                         waits=([(dve, ov_done[0]), (act, v_ready[0])]
                                if pkc == 0 else ()),
                         inc=(pe if s2 == 1 else None))
            pb_ring.release(pps, [(pe, grp)])
            pb_prev[0] = None
            lr = B.op("vector", lambda e: e.tensor_scalar_add(
                LROW[0:1, :], PS1[0:1, 1536:2048], 1e-38),
                waits=[(pe, grp)], inc=dve)
            lrow_done[0] = lr
            l1 = B.op("scalar", lambda e: e.activation(
                LROW[0:1, :], LROW[0:1, :], AF.Ln), waits=[(dve, lr)], inc=act)
            rc = B.op("scalar", lambda e: e.activation(
                LRB[0:1, :], LROW[0:1, :], AF.Exp, scale=-1.0), inc=act)
            lb = MM(PS1[:, 1536:2048], ONER_B[:], LRB[0:1, :], True, True,
                    waits=[(act, rc), (dve, lbs_used[0]), (dve, lr)], inc=pe)
            lv = B.op("vector", lambda e: e.tensor_copy(
                LBS[:, :], PS1[:, 1536:2048]), waits=[(pe, lb)], inc=dve)
            lbs_used[0] = lv
            os_, ow = ost_ring.next()
            ov = B.op("vector", lambda e, o=os_: e.tensor_tensor(
                OSTB[:, o, :], PS1[:, 1024:1536], LBS[:, :], ALU.mult),
                waits=[(pe, grp)] + ow, inc=dve)
            ov_done[0] = ov
            odw = ostw.load_slot(os_, lambda e, o=os_, b=b, h=h, qg=qg:
                                 e.dma_start(
                out=ago_in[h * 128:(h + 1) * 128,
                           b * SEQ + qg * 512: b * SEQ + (qg + 1) * 512],
                in_=OSTB[:, o, :]), waits=[(dve, ov)])
            ost_ring.release(os_, odw)

        for b in range(NB):
            emit_proj(b)
            for h in range(2):
                for qg in range(8):
                    emit_passB_qg(b, h, qg)

        ag2 = B.op("gpsimd", lambda e: e.collective_compute(
            "AllGather", ALU.bypass, replica_groups=[list(range(NCORES))],
            ins=[ago_in[:, :].opt()], outs=[ago_out[:, :].opt()]),
            waits=ostw.totals(), inc=cc)
        B.fence()
        pid_cell = []
        dom = B.new_ctr("dom")

        def om_read(e, dv):
            if not pid_cell:
                pid_cell.append(e.partition_id())
            return e.dma_start(
                out=OM[:, dv, :],
                in_=ago_out[dv * 128:(dv + 1) * 128, bass.ts(pid_cell[0], TC)])

        for dv in range(DC):
            B.dma_on("gpsimd", dom, lambda e, dv=dv: om_read(e, dv),
                     waits=[(cc, ag2)])
        om_done = (dom, dom.n)
        wos = DmaRing(B, "sync", "wos", 4)
        wo_evac = [p2evac[0]]
        for dog in range(4):
            grp = 0
            for dv in range(DC):
                s = wos.i % wos.depth
                _, ldw = wos.load(lambda e, dv=dv, s=s: e.dma_start(
                    out=WOS[:, s, :], in_=woS[dv * 128:(dv + 1) * 128, :]),
                    waits=[om_done, wosload])
                for dol in range(4):
                    do = dog * 4 + dol
                    for tq in range(2):
                        bk = dol * 2 + tq
                        ps = b1(bk) if bk < 4 else b2(bk - 4)
                        grp = MM(ps, WOS[:, s, do * 128:(do + 1) * 128],
                                 OM[:, dv, tq * 512:(tq + 1) * 512],
                                 start=(dv == 0), stop=(dv == DC - 1),
                                 waits=(ldw + [(dve, wo_evac[0])]
                                        if (dol == 0 and tq == 0) else ldw),
                                 inc=(pe if (dol == 3 and tq == 1) else None))
                wos.release(s, [(pe, grp)])
            for dol in range(4):
                do = dog * 4 + dol
                for tq in range(2):
                    bk = dol * 2 + tq
                    ps = b1(bk) if bk < 4 else b2(bk - 4)
                    B.op("vector", lambda e, do=do, tq=tq, ps=ps:
                         e.tensor_tensor(
                        xsl(do, tq), ps, xsl(do, tq), ALU.add),
                        waits=[(pe, grp)], inc=dve)
            wo_evac[0] = dve.n
        p2evac[0] = dve.n
        return dve.n

    emit_qshard()
    ln1 = emit_layernorm([xw])
    if _TAPS:
        B.dma("gpsimd", lambda e: e.dma_start(out=t_thp[:, :], in_=THP[:]),
              waits=[(dve, gamma_done)])
        B.dma("gpsimd", lambda e: e.dma_start(out=t_xln[:, :], in_=XLN[:]),
              waits=[(dve, ln1)])
    def _ln2a_hook(tq0done):
        if _STAGE < 2:
            return None
        ln2a = emit_layernorm([(dve, tq0done)], (0,), alt=True)
        emit_ag1_half(0, ln2a)
        return ln2a

    out_src = emit_ffn(True, ln1, mid_hook=_ln2a_hook)
    if _TAPS:
        B.dma("gpsimd", lambda e: e.dma_start(out=t_xa[:, :], in_=X[:, 0:1024]),
              waits=[(dve, _dtb0_snap[0])])
        B.dma("gpsimd", lambda e: e.dma_start(out=t_w1q[:, :], in_=w1s[0, :, :]),
              waits=[(dve, out_src)])
        B.dma("gpsimd", lambda e: e.dma_start(out=t_w3q[:, :], in_=w3s[0, :, :]),
              waits=[(dve, out_src)])
        B.dma("gpsimd", lambda e: e.dma_start(out=t_hid[:, :], in_=HID[:, 0, :]),
              waits=[(dve, out_src)])
    if _STAGE >= 2:
        ln2b = emit_layernorm([(dve, out_src)], (1,), alt=True)
        emit_ag1_half(1, ln2b)
        out_src = emit_attention()
    if _STAGE >= 3:
        B.fence()
        ln3 = emit_layernorm([(dve, out_src)], alt=True)
        out_src = emit_ffn(False, ln3)

    final_dve = dve.n
    dy = B.new_ctr("dy")
    for dc in range(DC):
        B.dma_on("sync", dy, lambda e, dc=dc: e.dma_start(
            out=y[dc * 128:(dc + 1) * 128, :],
            in_=X[:, dc * TC:(dc + 1) * TC]), waits=[(dve, final_dve)])
    out_done = dy.n
    B.op("sync", lambda e: e.wait_ge(dy.h, out_done))
    if _TAPS:
        B.op("gpsimd", lambda e: e.wait_ge(dG.h, dG.n))

    with nc.Block() as block:
        block.sync(lambda e: B.replay("sync", e))
        block.scalar(lambda e: B.replay("scalar", e))
        block.gpsimd(lambda e: B.replay("gpsimd", e))
        block.tensor(lambda e: B.replay("tensor", e))
        block.vector(lambda e: B.replay("vector", e))

    stack.__exit__(None, None, None)
    return nc


_NEG = np.float32(-1e30)


def _masks():
    qi = np.arange(128)[:, None]
    kk = np.arange(512)[None, :]
    ma = np.concatenate(
        [np.where(kk <= o * 128 + qi, np.float32(0.0), _NEG)
         for o in range(4)], axis=1).astype(ml_dtypes.bfloat16)
    ki = np.arange(128)[:, None]
    qq = np.arange(512)[None, :]
    mb = np.concatenate(
        [np.where(qq >= v * 128 + ki, np.float32(0.0), _NEG)
         for v in range(4)], axis=1).astype(ml_dtypes.bfloat16)
    return ma, mb


def kernel(L0_attn_x, L0_attn_Wq, L0_attn_Wk, L0_attn_Wv, L0_attn_Wo,
           L0_attn_gq, L0_attn_gk, L0_ffn_w1, L0_ffn_w2, L0_ffn_w3):
    f = np.float32
    x = np.asarray(L0_attn_x, f)
    Wq = np.asarray(L0_attn_Wq, f); Wk = np.asarray(L0_attn_Wk, f)
    Wv = np.asarray(L0_attn_Wv, f); Wo = np.asarray(L0_attn_Wo, f)
    gq = np.asarray(L0_attn_gq, f); gk = np.asarray(L0_attn_gk, f)
    w1 = np.asarray(L0_ffn_w1, f); w2 = np.asarray(L0_ffn_w2, f)
    w3 = np.asarray(L0_ffn_w3, f)

    Ct = np.ascontiguousarray
    w1Tm = Ct(w1.T); w2Tm = Ct(w2.T); w3Tm = Ct(w3.T)
    WqT = Ct(Wq.T); WkT = Ct(Wk.T); WvT = Ct(Wv.T); WoT = Ct(Wo.T)
    ma, mb = _masks()
    shared = {
        "w1T": w1Tm, "w2T": w2Tm, "w3T": w3Tm, "woT": WoT,
        "gq": Ct(gq.reshape(HD, 1)), "gk": Ct(gk.reshape(HD, 1)),
        "maskB": mb,
        "ones_b": np.ones((128, 1), ml_dtypes.bfloat16),
        "ones_f": np.ones((128, 1), f),
        "onesr_f": np.ones((1, 128), f),
        "onesr_b": np.ones((1, 128), ml_dtypes.bfloat16),
        "ide": np.eye(128, dtype=f),
        "ide_b": np.eye(128, dtype=ml_dtypes.bfloat16),
    }
    in_maps = []
    for r in range(NCORES):
        b, c = r // 4, r % 4
        m = dict(shared)
        m["xT"] = Ct(x[b, c * TC:(c + 1) * TC, :].T)
        hcs = [h for h in range(r, HC, 8)]
        g1 = np.zeros((D, 768), f); g2 = np.zeros((D, 768), f)
        g3 = np.zeros((768, D), f)
        for i, h in enumerate(hcs):
            g1[:, i * 128:(i + 1) * 128] = w1Tm[:, h * 128:(h + 1) * 128]
            g2[:, i * 128:(i + 1) * 128] = w2Tm[:, h * 128:(h + 1) * 128]
            g3[i * 128:(i + 1) * 128, :] = w3Tm[h * 128:(h + 1) * 128, :]
        m["g1s"] = g1; m["g2s"] = g2; m["g3s"] = g3
        m["wqT"] = Ct(WqT[:, r * 256:(r + 1) * 256])
        m["wkT"] = Ct(WkT[:, r * 256:(r + 1) * 256])
        m["wvT"] = Ct(WvT[:, r * 256:(r + 1) * 256])
        in_maps.append(m)

    nc = build_program()
    res = run_bass_kernel_spmd(nc, in_maps, core_ids=list(range(NCORES)))
    out = np.empty((NB, SEQ, D), f)
    for r in range(NCORES):
        b, c = r // 4, r % 4
        out[b, c * TC:(c + 1) * TC, :] = np.asarray(res.results[r]["y"]).T
    if _TAPS:
        _sys.modules[__name__]._tap_results = res.results
    return out
